# version 11
# baseline (speedup 1.0000x reference)
import numpy as np
import ml_dtypes
from contextlib import ExitStack
import concourse.bass as bass
import concourse.mybir as mybir
from concourse.bass_utils import run_bass_kernel_spmd

F32 = mybir.dt.float32
BF16 = mybir.dt.bfloat16
AF = mybir.ActivationFunctionType
ALU = mybir.AluOpType
AX = mybir.AxisListType
BFNP = ml_dtypes.bfloat16

D = 1024
NH = 16
TSEQ = 4096
NSEQ = 2
QTOK = 1024
NTILE = 4
TILE = 256
NQ = NSEQ * 4
NEXP = 32
MOE_I = 512
ANC_I = 256
NANC = 16

import os
EMIT_MOE = os.environ.get("KMOE", "1") == "1"
N_Q_EMIT = int(os.environ.get("KNQ", "8"))
N_PAIR_EMIT = int(os.environ.get("KNP", "25"))
KSTAGE = int(os.environ.get("KSTAGE", "9"))
KDBG = os.environ.get("KDBG", "0") == "1"
KSCAN = int(os.environ.get("KSCAN", "9"))


class Ctr:
    def __init__(self, stk, nc, name):
        self.h = stk.enter_context(nc.semaphore(name))
        self.val = 0

    def inc(self, inst, n=1):
        inst.then_inc(self.h, n)
        self.val += n
        return self.val


def build_program():
    nc = bass.Bass("TRN2", target_bir_lowering=False, detect_race_conditions=False)
    stk = ExitStack()

    x_d = nc.dram_tensor("x", [NSEQ, TSEQ, D], F32, kind="ExternalInput").ap()
    out_d = nc.dram_tensor("out", [NSEQ, TSEQ, D], F32, kind="ExternalOutput").ap()
    if KDBG:
        dbgb_d = nc.dram_tensor("dbgb", [128, 8192], BF16, kind="ExternalOutput").ap()
        dbgf_d = nc.dram_tensor("dbgf", [128, 1024], F32, kind="ExternalOutput").ap()
        dbgb2_d = nc.dram_tensor("dbgb2", [128, 2048], BF16, kind="ExternalOutput").ap()

    wq_d = nc.dram_tensor("Wq", [D, D], F32, kind="ExternalInput").ap()
    wk_d = nc.dram_tensor("Wk", [D, D], F32, kind="ExternalInput").ap()
    wv_d = nc.dram_tensor("Wv", [D, D], F32, kind="ExternalInput").ap()
    wo_d = nc.dram_tensor("Wo", [D, D], F32, kind="ExternalInput").ap()
    bq_d = nc.dram_tensor("bq", [D], F32, kind="ExternalInput").ap()
    bk_d = nc.dram_tensor("bk", [D], F32, kind="ExternalInput").ap()
    bv_d = nc.dram_tensor("bv", [D], F32, kind="ExternalInput").ap()
    bo_d = nc.dram_tensor("bo", [D], F32, kind="ExternalInput").ap()
    gw_d = nc.dram_tensor("gate_w", [NEXP, D], F32, kind="ExternalInput").ap()
    gb_d = nc.dram_tensor("gate_b", [NEXP], F32, kind="ExternalInput").ap()
    wg_d = nc.dram_tensor("w_gate", [NEXP, D, MOE_I], F32, kind="ExternalInput").ap()
    wu_d = nc.dram_tensor("w_up", [NEXP, D, MOE_I], F32, kind="ExternalInput").ap()
    wd_d = nc.dram_tensor("w_down", [NEXP, MOE_I, D], F32, kind="ExternalInput").ap()
    wsg_d = nc.dram_tensor("sw_gate", [D, MOE_I], F32, kind="ExternalInput").ap()
    wsu_d = nc.dram_tensor("sw_up", [D, MOE_I], F32, kind="ExternalInput").ap()
    wsd_d = nc.dram_tensor("sw_down", [MOE_I, D], F32, kind="ExternalInput").ap()
    wag_d = nc.dram_tensor("aw_gate", [NANC, D, ANC_I], F32, kind="ExternalInput").ap()
    wau_d = nc.dram_tensor("aw_up", [NANC, D, ANC_I], F32, kind="ExternalInput").ap()
    wad_d = nc.dram_tensor("aw_down", [NANC, ANC_I, D], F32, kind="ExternalInput").ap()

    identb_d = nc.dram_tensor("identb", [128, 128], BF16, kind="ExternalInput").ap()
    identf_d = nc.dram_tensor("identf", [128, 128], F32, kind="ExternalInput").ap()
    maskT_d = nc.dram_tensor("maskT", [128, 128], F32, kind="ExternalInput").ap()
    onesc_d = nc.dram_tensor("onesc", [128, 64], BF16, kind="ExternalInput").ap()
    onesr_d = nc.dram_tensor("onesr", [1, 128], BF16, kind="ExternalInput").ap()
    onesrf_d = nc.dram_tensor("onesrf", [1, 128], F32, kind="ExternalInput").ap()
    oh32_d = nc.dram_tensor("oh32", [32, 32 * 128], BF16, kind="ExternalInput").ap()
    oh16_d = nc.dram_tensor("oh16", [16, 16 * 128], BF16, kind="ExternalInput").ap()

    scr_attn = nc.dram_tensor("scr_attn", [4, D, D], BF16, kind="Internal").ap()
    scr_wg = nc.dram_tensor("scr_wg", [NEXP, 128, 8 * MOE_I], BF16, kind="Internal").ap()
    scr_wu = nc.dram_tensor("scr_wu", [NEXP, 128, 8 * MOE_I], BF16, kind="Internal").ap()
    scr_wd = nc.dram_tensor("scr_wd", [NEXP, 128, 4 * D], BF16, kind="Internal").ap()
    scr_wsg = nc.dram_tensor("scr_wsg", [128, 8 * MOE_I], BF16, kind="Internal").ap()
    scr_wsu = nc.dram_tensor("scr_wsu", [128, 8 * MOE_I], BF16, kind="Internal").ap()
    scr_wsd = nc.dram_tensor("scr_wsd", [128, 4 * D], BF16, kind="Internal").ap()
    scr_wag = nc.dram_tensor("scr_wag", [NANC, 128, 8 * ANC_I], BF16, kind="Internal").ap()
    scr_wau = nc.dram_tensor("scr_wau", [NANC, 128, 8 * ANC_I], BF16, kind="Internal").ap()
    scr_wad = nc.dram_tensor("scr_wad", [NANC, 128, 2 * D], BF16, kind="Internal").ap()

    sb = lambda n, s, d: stk.enter_context(nc.sbuf_tensor(n, s, d))
    y1q = sb("y1q", [128, 8 * QTOK], F32)
    y1qbf = sb("y1qbf", [128, 8 * QTOK], BF16)
    kv_f = sb("kv_f", [128, 512], F32)
    kv_bfe = sb("kv_bfe", [128, 512], BF16)
    kv_bfo = sb("kv_bfo", [128, 512], BF16)
    ksum = sb("ksum", [128, 8], F32)
    ksum_d = sb("ksum_d", [128, 8], F32)
    ksbce = sb("ksbce", [128, 512], BF16)
    ksbco = sb("ksbco", [128, 512], BF16)
    identb = sb("identb_s", [128, 128], BF16)
    identf = sb("identf_s", [128, 128], F32)
    maskT = sb("maskT_s", [128, 128], F32)
    onesc = sb("onesc_s", [128, 64], BF16)
    onesr = sb("onesr_s", [1, 128], BF16)
    onesrf = sb("onesrf_s", [1, 128], F32)
    bias_c = sb("bias_c", [128, 32], F32)
    bv_rbf = sb("bv_rbf", [1, D], BF16)
    gwT = sb("gwT", [128, 8 * 32], F32)
    gb_col = sb("gb_col", [32, 1], F32)
    eps6 = sb("eps6", [128, 1], F32)
    eps5 = sb("eps5", [128, 1], F32)
    junk_a = sb("junk_a", [1, 8], F32)
    junk_v = sb("junk_v", [1, 8], F32)
    cw_em = sb("cw_em", [32, QTOK], BF16)
    acw_em = sb("acw_em", [16, QTOK], BF16)

    C = lambda n: Ctr(stk, nc, n)
    bar = C("bar")
    s_const = C("s_const")
    s_wA = [C("s_wA0"), C("s_wA1")]
    s_x = C("s_x")
    s_wb = [C("s_wb0"), C("s_wb1"), C("s_wb2"), C("s_wb3")]
    s_wc = [C("s_wc0"), C("s_wc1"), C("s_wc2")]
    s_scr = C("s_scr")
    s_st = [C("s_st0"), C("s_st1")]
    s_oh = C("s_oh")
    pe_xt = C("pe_xt"); dve_xt = C("dve_xt"); dve_xbf = C("dve_xbf")
    pe_phi = C("pe_phi"); act_phi = C("act_phi"); dve_phi = C("dve_phi")
    pe_vt = C("pe_vt"); act_vt = C("act_vt")
    pe_kt = C("pe_kt"); pe_st = C("pe_st"); pe_num = C("pe_num")
    pe_den = C("pe_den"); pe_kv = C("pe_kv")
    dve_smt = C("dve_smt"); dve_kt = C("dve_kt"); dve_attn = C("dve_attn")
    dve_state = C("dve_state"); act_den = C("act_den")
    pe_o = C("pe_o"); dve_r1 = C("dve_r1")
    pe_ln = C("pe_ln"); act_ln = C("act_ln"); dve_ln = C("dve_ln")
    pe_rt = C("pe_rt"); act_rt = C("act_rt"); dve_rt = C("dve_rt")
    pe_cwb = C("pe_cwb"); pe_gu = C("pe_gu"); pe_down = C("pe_down"); pe_g = C("pe_g")
    act_silu = C("act_silu"); dve_h = C("dve_h"); dve_evac = C("dve_evac")
    dve_cwb = C("dve_cwb")
    pe_ot = C("pe_ot"); dve_ot = C("dve_ot")
    pe_pro = C("pe_pro"); dve_pro = C("dve_pro")

    bar_n = [0]
    block = stk.enter_context(nc.Block())

    def barrier():
        bar_n[0] += 1
        tgt = 5 * bar_n[0]
        def _s(sync):
            sync.nop().then_inc(bar.h, 1)
            sync.wait_ge(bar.h, tgt)
        block.sync(_s)
        def _g(gp):
            gp.nop().then_inc(bar.h, 1)
            gp.wait_ge(bar.h, tgt)
        block.gpsimd(_g)
        def _t(t):
            t.nop().then_inc(bar.h, 1)
            t.wait_ge(bar.h, tgt)
        block.tensor(_t)
        def _a(a):
            a.copy(out=junk_a[0:1, 0:1], in_=junk_a[0:1, 1:2]).then_inc(bar.h, 1)
            a.wait_ge(bar.h, tgt)
        block.scalar(_a)
        def _v(v):
            v.tensor_copy(out=junk_v[0:1, 2:3], in_=junk_v[0:1, 3:4]).then_inc(bar.h, 1)
            v.wait_ge(bar.h, tgt)
        block.vector(_v)

    subs = [(0, e) for e in range(NEXP)] + [(1, 0)] + [(2, l) for l in range(NANC)]
    pair_of = []
    i = 0
    while i < len(subs):
        if subs[i][0] == 1:
            pair_of.append((i, 1)); i += 1
        else:
            pair_of.append((i, 2)); i += 2
    NSUBS = len(subs)

    def sub_srcs(kind, idx):
        if kind == 0:
            return (wg_d[idx], wu_d[idx], wd_d[idx],
                    scr_wg[idx], scr_wu[idx], scr_wd[idx], MOE_I)
        if kind == 1:
            return (wsg_d, wsu_d, wsd_d, scr_wsg, scr_wsu, scr_wsd, MOE_I)
        return (wag_d[idx], wau_d[idx], wad_d[idx],
                scr_wag[idx], scr_wau[idx], scr_wad[idx], ANC_I)

    cast_grp = {}
    with nc.sbuf_tensor("pro_stage", [128, 8192], BF16) as pro_stage, \
         nc.sbuf_tensor("pro_stage2", [128, 8192], BF16) as pro_stage2, \
         nc.sbuf_tensor("pro_bias", [8, 128], F32) as pro_bias, \
         nc.sbuf_tensor("pro_gw", [32, D], F32) as pro_gw, \
         nc.sbuf_tensor("pro_bv", [1, D], F32) as bv_row, \
         nc.psum_tensor("ps_pro", [128, 256], F32) as ps_pro:

        def _s(sync):
            for src, dst in [(identb_d, identb), (identf_d, identf), (maskT_d, maskT),
                             (onesc_d, onesc), (onesr_d, onesr), (onesrf_d, onesrf)]:
                sync.dma_start(out=dst[:], in_=src).then_inc(s_const.h, 16)
                s_const.val += 16
            for i, bsrc in enumerate([bq_d, bk_d, bo_d]):
                sync.dma_start(out=pro_bias[:], in_=bsrc.rearrange("(c p) -> c p", p=128)
                               ).then_inc(s_const.h, 16)
                s_const.val += 16
                sync.wait_ge(pe_pro.h, i + 1)
            sync.dma_start(out=bv_row[:], in_=bv_d.rearrange("(one d) -> one d", one=1)
                           ).then_inc(s_const.h, 16)
            s_const.val += 16
            sync.dma_start(out=gb_col[:], in_=gb_d.rearrange("(e one) -> e one", one=1)
                           ).then_inc(s_const.h, 16)
            s_const.val += 16
            sync.dma_start(out=pro_gw[:], in_=gw_d).then_inc(s_const.h, 16)
            s_const.val += 16
            for w in range(4):
                sync.wait_ge(s_wc[0].h, (w + 1) * 16)
                sync.dma_start(out=scr_attn[w].rearrange("(c p) j -> p c j", p=128),
                               in_=(pro_stage if w % 2 == 0 else pro_stage2)[:]
                                   .rearrange("p (c j) -> p c j", j=D)
                               ).then_inc(s_scr.h, 16)
                s_scr.val += 16
        block.sync(_s)

        def _g(gp):
            for w, wsrc in enumerate([wq_d, wk_d, wv_d, wo_d]):
                if w >= 2:
                    gp.wait_ge(s_scr.h, (w - 1) * 16)
                gp.dma_start(out=(pro_stage if w % 2 == 0 else pro_stage2)[:]
                                 .rearrange("p (c j) -> p c j", j=D),
                             in_=wsrc.rearrange("(c p) j -> p c j", p=128)
                             ).then_inc(s_wc[0].h, 16)
                s_wc[0].val += 16
            for s_i, (kind, idx) in enumerate(subs):
                sg, su, sdn, cg, cu, cd, n_i = sub_srcs(kind, idx)
                grp = 0 if s_i < 4 else (1 if s_i < 20 else 2)
                sem = s_wc[grp]
                for src, dst, inner in [(sg, cg, n_i), (su, cu, n_i), (sdn, cd, 1024)]:
                    gp.dma_start(out=dst.rearrange("p (c i) -> p c i", i=inner),
                                 in_=src.rearrange("(c p) i -> p c i", p=128)
                                 ).then_inc(sem.h, 16)
                    sem.val += 16
                cast_grp[s_i] = (sem, sem.val)
        block.gpsimd(_g)

        def _t(t):
            for i in range(3):
                t.wait_ge(s_const.h, (6 + i + 1) * 16)
                if i > 0:
                    t.wait_ge(dve_pro.h, i)
                tr = t.transpose(out=ps_pro[:, 0:8], in_=pro_bias[0:8, :],
                                 identity=identf[0:8, 0:8])
                pe_pro.inc(tr)
            t.wait_ge(s_const.h, s_const.val)
            for kc in range(8):
                t.wait_ge(dve_pro.h, 3 + kc)
                tr = t.transpose(out=ps_pro[:, 32:64], in_=pro_gw[:, kc * 128:(kc + 1) * 128],
                                 identity=identf[0:32, 0:32])
                pe_pro.inc(tr)
        block.tensor(_t)

        def _a0(a):
            a.memzero(junk_a[:])
        block.scalar(_a0)

        def _v(v):
            v.memset(junk_v[:], 0.0)
            v.memset(eps6[:], 1e-6)
            v.memset(eps5[:], 1e-5)
            for i in range(3):
                v.wait_ge(pe_pro.h, i + 1)
                cp = v.tensor_copy(out=bias_c[:, i * 8:(i + 1) * 8], in_=ps_pro[:, 0:8])
                dve_pro.inc(cp)
            for kc in range(8):
                v.wait_ge(pe_pro.h, 3 + kc + 1)
                cp = v.tensor_copy(out=gwT[:, kc * 32:(kc + 1) * 32], in_=ps_pro[:, 32:64])
                dve_pro.inc(cp)
            v.wait_ge(s_const.h, s_const.val)
            v.tensor_copy(out=bv_rbf[:], in_=bv_row[:])
        block.vector(_v)

        barrier()

    hb = lambda h: (h % 2) * 64
    hcf = lambda h: h // 2
    bq_col = lambda oc: bias_c[:, oc:oc + 1]
    bk_col = lambda oc: bias_c[:, 8 + oc:8 + oc + 1]
    bo_col = lambda oc: bias_c[:, 16 + oc:16 + oc + 1]

    down_after = {}
    scr_after = {}
    wval = {}
    for q in range(min(NQ, N_Q_EMIT)):
        seq, qq = q // 4, q % 4
        qbase = qq * QTOK
        first_q = (q == 0)

        A_ = ExitStack()
        _asb = lambda n, sh, dt: A_.enter_context(nc.sbuf_tensor(n, sh, dt))
        wqT = _asb(f"wqT{q}", [128, 8 * D], BF16)
        wkT = _asb(f"wkT{q}", [128, 8 * D], BF16)
        wvT = _asb(f"wvT{q}", [128, 8 * D], BF16)
        woT = _asb(f"woT{q}", [128, 8 * D], BF16)
        x_tm = _asb(f"x_tm{q}", [128, 2048], F32)
        x_fm = _asb(f"x_fm{q}", [128, 2048], F32)
        x_bf = _asb(f"x_bf{q}", [128, 2048], BF16)
        q_bf = _asb(f"q_bf{q}", [128, 2048], BF16)
        k_bfe = _asb(f"k_bfe{q}", [128, 2048], BF16)
        k_bfo = _asb(f"k_bfo{q}", [128, 2048], BF16)
        vt_bf = _asb(f"vt_bf{q}", [128, 2048], BF16)
        kt_bf = _asb(f"kt_bf{q}", [128, 2048], BF16)
        smt_sb = _asb(f"smt{q}", [128, 2048], BF16)
        rbc_sb = _asb(f"rbc{q}", [128, 1024], F32)
        phi_a = _asb(f"phia{q}", [128, 2048], F32)
        phi_b = _asb(f"phib{q}", [128, 2048], F32)
        sq_bf = _asb(f"sqbf{q}", [128, 2048], BF16)
        attn_bf = _asb(f"attnbf{q}", [128, 2048], BF16)
        rows = _asb(f"rows{q}", [1, 1536], F32)
        if True:
            wa = s_wA[q % 2]
            def _sW(sync, wa=wa, first_q=first_q, wqT=wqT, wkT=wkT, wvT=wvT, woT=woT):
                for wi, wT in enumerate([wqT, wkT, wvT, woT]):
                    if first_q:
                        sync.wait_ge(s_scr.h, (wi + 1) * 16)
                    for kc in range(8):
                        sync.dma_start(out=wT[:, kc * D:(kc + 1) * D],
                                       in_=scr_attn[wi][:, kc * 128:(kc + 1) * 128],
                                       transpose=True).then_inc(wa.h, 16)
                        wa.val += 16
            block.sync(_sW)
            wA_done = wa.val
            w_done = [wa.val - 24 * 16, wa.val - 16 * 16, wa.val - 8 * 16, wa.val]

            for tile in range(NTILE):
                gt = q * NTILE + tile
                tok0 = tile * TILE
                atok = qbase + tok0
                y1reg = y1q[:].rearrange("p (c t) -> p c t", t=QTOK)[:, :, tok0:tok0 + TILE]
                y1breg = y1qbf[:].rearrange("p (c t) -> p c t", t=QTOK)[:, :, tok0:tok0 + TILE]

                S13 = ExitStack()
                ps_qk = S13.enter_context(nc.psum_tensor(f"psqk{q}_{tile}", [128, 2048], F32))
                ps_vt = S13.enter_context(nc.psum_tensor(f"psvt{q}_{tile}", [128, 2048], F32))
                if True:

                    def _g1(gp, atok=atok, gt=gt):
                        if gt > 0:
                            gp.wait_ge(pe_xt.h, 16 * gt)
                        for tc in range(2):
                            gp.dma_start(out=x_tm[:, tc * 1024:(tc + 1) * 1024],
                                         in_=x_d[seq, atok + tc * 128: atok + (tc + 1) * 128, :]
                                         ).then_inc(s_x.h, 16)
                            s_x.val += 16
                    block.gpsimd(_g1)

                    dve_xt_base = dve_xt.val
                    dve_r1_now = dve_r1.val
                    dve_ln_now = dve_ln.val
                    def _t1(t, gt=gt, dve_xt_base=dve_xt_base,
                            dve_r1_now=dve_r1_now, dve_ln_now=dve_ln_now):
                        t.wait_ge(s_x.h, s_x.val)
                        if gt > 0:
                            t.wait_ge(dve_r1.h, dve_r1_now)
                            t.wait_ge(dve_ln.h, dve_ln_now)
                        n = 0
                        for tc in range(2):
                            for fc in range(8):
                                g4 = n // 4
                                if g4 >= 2:
                                    t.wait_ge(dve_xt.h, dve_xt_base + g4 - 1)
                                tr = t.transpose(
                                    out=ps_qk[:, (n % 8) * 128:(n % 8) * 128 + 128],
                                    in_=x_tm[:, tc * 1024 + fc * 128: tc * 1024 + (fc + 1) * 128],
                                    identity=identf[:])
                                if n % 4 == 3:
                                    pe_xt.inc(tr, 4)
                                n += 1
                    block.tensor(_t1)

                    def _v1(v, gt=gt):
                        for g4 in range(4):
                            v.wait_ge(pe_xt.h, 16 * gt + (g4 + 1) * 4)
                            tc = g4 // 2
                            fc0 = (g4 % 2) * 4
                            cp = v.tensor_copy(
                                out=x_fm[:].rearrange("p (c t) -> p c t", t=256)
                                    [:, fc0:fc0 + 4, tc * 128:(tc + 1) * 128],
                                in_=ps_qk[:, (g4 % 2) * 512:(g4 % 2) * 512 + 512]
                                    .rearrange("p (c t) -> p c t", t=128))
                            dve_xt.inc(cp)
                        cb = v.tensor_copy(out=x_bf[:], in_=x_fm[:])
                        dve_xbf.inc(cb)
                    block.vector(_v1)

                    if KSTAGE < 2:
                        S13.close()
                        continue
                    act_phi_base = act_phi.val
                    dve_phi_base = dve_phi.val
                    def _t2(t, gt=gt, act_phi_base=act_phi_base, dve_phi_base=dve_phi_base,
                            wqT=wqT, wkT=wkT):
                        t.wait_ge(wa.h, w_done[0])
                        t.wait_ge(dve_xbf.h, gt + 1)
                        for pi, wT in enumerate([wqT, wkT]):
                            if pi == 1:
                                t.wait_ge(wa.h, w_done[1])
                            for oc in range(8):
                                if pi == 1:
                                    t.wait_ge(act_phi.h, act_phi_base + (oc // 2) * 4 + 2)
                                    t.wait_ge(dve_phi.h, dve_phi_base + (oc // 2) * 4 + 2)
                                for kc in range(8):
                                    mm = t.matmul(out=ps_qk[:, oc * 256:(oc + 1) * 256],
                                                  lhsT=wT[:, kc * D + oc * 128: kc * D + (oc + 1) * 128],
                                                  rhs=x_bf[:, kc * 256:(kc + 1) * 256],
                                                  start=(kc == 0), stop=(kc == 7))
                                pe_phi.inc(mm)
                    block.tensor(_t2)

                    pe_phi_base = pe_phi.val - 16
                    def _a2(a, base=pe_phi_base, act_phi_base=act_phi_base,
                            dve_phi_base=dve_phi_base):
                        for pi in range(2):
                            for pr in range(4):
                                for half in range(2):
                                    oc = pr * 2 + half
                                    bias = bq_col(oc) if pi == 0 else bk_col(oc)
                                    a.wait_ge(pe_phi.h, base + pi * 8 + (oc | 1) + 1)
                                    r = a.activation(out=phi_a[:, oc * 256:(oc + 1) * 256],
                                                     in_=ps_qk[:, oc * 256:(oc + 1) * 256],
                                                     func=AF.Relu, bias=bias)
                                    act_phi.inc(r)
                                for half in range(2):
                                    oc = pr * 2 + half
                                    a.wait_ge(dve_phi.h,
                                              dve_phi_base + pi * 16 + pr * 4 + half + 1)
                                    e = a.activation(out=phi_b[:, oc * 256:(oc + 1) * 256],
                                                     in_=phi_b[:, oc * 256:(oc + 1) * 256],
                                                     func=AF.Exp)
                                    act_phi.inc(e)
                    block.scalar(_a2)

                    def _v2(v, base=pe_phi_base, act_phi_base=act_phi_base,
                            q_bf=q_bf, k_bfe=k_bfe, k_bfo=k_bfo, tile=tile):
                        if tile == 0:
                            v.memset(k_bfe[64:128, :], 0.0)
                            v.memset(k_bfo[0:64, :], 0.0)
                        for pi, dst in [(0, q_bf), (1, None)]:
                            for pr in range(4):
                                for half in range(2):
                                    oc = pr * 2 + half
                                    bias = bq_col(oc) if pi == 0 else bk_col(oc)
                                    v.wait_ge(pe_phi.h, base + pi * 8 + (oc | 1) + 1)
                                    v.wait_ge(act_phi.h, act_phi_base + pi * 16 + pr * 4 + 2)
                                    m = v.tensor_scalar(out=phi_b[:, oc * 256:(oc + 1) * 256],
                                                        in0=ps_qk[:, oc * 256:(oc + 1) * 256],
                                                        scalar1=bias, scalar2=0.0,
                                                        op0=ALU.add, op1=ALU.min)
                                    dve_phi.inc(m)
                                for half in range(2):
                                    oc = pr * 2 + half
                                    v.wait_ge(act_phi.h,
                                              act_phi_base + pi * 16 + pr * 4 + 3 + half)
                                    if pi == 0:
                                        sdd = v.tensor_add(out=dst[:, oc * 256:(oc + 1) * 256],
                                                           in0=phi_a[:, oc * 256:(oc + 1) * 256],
                                                           in1=phi_b[:, oc * 256:(oc + 1) * 256])
                                    else:
                                        v.tensor_add(out=k_bfe[0:64, oc * 256:(oc + 1) * 256],
                                                     in0=phi_a[0:64, oc * 256:(oc + 1) * 256],
                                                     in1=phi_b[0:64, oc * 256:(oc + 1) * 256])
                                        sdd = v.tensor_add(out=k_bfo[64:128, oc * 256:(oc + 1) * 256],
                                                           in0=phi_a[64:128, oc * 256:(oc + 1) * 256],
                                                           in1=phi_b[64:128, oc * 256:(oc + 1) * 256])
                                    dve_phi.inc(sdd)
                    block.vector(_v2)

                    def _t3(t, wvT=wvT):
                        t.wait_ge(wa.h, w_done[2])
                        for tc in range(2):
                            for half in range(2):
                                sl = tc * 2 + half
                                for kc in range(8):
                                    t.matmul(out=ps_vt[:, sl * 512:(sl + 1) * 512],
                                             lhsT=x_bf[:, kc * 256 + tc * 128: kc * 256 + (tc + 1) * 128],
                                             rhs=wvT[:, kc * D + half * 512: kc * D + (half + 1) * 512],
                                             start=(kc == 0), stop=False)
                                mm = t.matmul(out=ps_vt[:, sl * 512:(sl + 1) * 512],
                                              lhsT=onesr[:],
                                              rhs=bv_rbf[0:1, half * 512:(half + 1) * 512],
                                              start=False, stop=True)
                                pe_vt.inc(mm)
                    block.tensor(_t3)

                    pe_vt_base = pe_vt.val - 4
                    def _a3(a, base=pe_vt_base, vt_bf=vt_bf):
                        for sl in range(4):
                            a.wait_ge(pe_vt.h, base + sl + 1)
                            tc, half = sl // 2, sl % 2
                            c = a.activation(out=vt_bf[:, tc * 1024 + half * 512:
                                                       tc * 1024 + (half + 1) * 512],
                                             in_=ps_vt[:, sl * 512:(sl + 1) * 512],
                                             func=AF.Identity)
                            act_vt.inc(c)
                    block.scalar(_a3)

                S13.close()
                if KSTAGE < 4:
                    continue
                S4_ = ExitStack()
                ps_sT = S4_.enter_context(nc.psum_tensor(f"psst{q}_{tile}", [128, 2048], F32))
                ps_n = S4_.enter_context(nc.psum_tensor(f"psn{q}_{tile}", [128, 1024], F32))
                ps_d = ps_sT
                ps_kt = S4_.enter_context(nc.psum_tensor(f"pskt{q}_{tile}", [128, 1024], BF16))
                ps_kt2 = S4_.enter_context(nc.psum_tensor(f"pskt2{q}_{tile}", [128, 1024], BF16))
                ps_kv = ps_sT[:, 1024:1536]
                if True:

                    for tc in range(2):
                        cg = q * 8 + tile * 2 + tc
                        smt_fut = dve_smt.val + 1
                        kt_fut = dve_kt.val + 2
                        first_scan = (tc == 0 and tile == 0 and qq == 0)
                        state_now = dve_state.val + (1 if first_scan else 0)
                        act_vt_now = act_vt.val
                        dve_phi_now = dve_phi.val
                        act_phi_now = act_phi.val
                        dve_attn_now = dve_attn.val
                        act_den_now = act_den.val

                        def _t4(t, tc=tc, cg=cg, tile=tile, smt_fut=smt_fut, kt_fut=kt_fut,
                                _dump_gate=(KDBG and q == 0 and tile == 0 and tc == 1),
                                state_now=state_now, act_vt_now=act_vt_now,
                                dve_phi_now=dve_phi_now, act_phi_now=act_phi_now,
                                dve_attn_now=dve_attn_now, act_den_now=act_den_now,
                                k_bfe=k_bfe, k_bfo=k_bfo, q_bf=q_bf, vt_bf=vt_bf,
                                kt_bf=kt_bf, smt_sb=smt_sb):
                            if tc == 0:
                                t.wait_ge(act_vt.h, act_vt_now)
                                t.wait_ge(dve_phi.h, dve_phi_now)
                                t.wait_ge(act_phi.h, act_phi_now)
                            if _dump_gate:
                                t.wait_ge(s_st[0].h, 9 * 16)
                            if cg > 0:
                                if KSCAN >= 5:
                                    t.wait_ge(dve_state.h, state_now)
                                if KSCAN >= 2:
                                    t.wait_ge(dve_smt.h, smt_fut - 1)
                                if KSCAN >= 1:
                                    t.wait_ge(dve_kt.h, kt_fut - 2)
                                if KSCAN >= 4:
                                    t.wait_ge(dve_attn.h, dve_attn_now)
                                if KSCAN >= 3:
                                    t.wait_ge(act_den.h, act_den_now)
                            if KSCAN >= 1:
                              for fc in range(8):
                                t.transpose(out=ps_kt[0:64, fc * 128:(fc + 1) * 128].bitcast(BF16)
                                            if False else ps_kt[:, fc * 128:(fc + 1) * 128],
                                            in_=k_bfe[:, fc * 256 + tc * 128: fc * 256 + (tc + 1) * 128],
                                            identity=identb[:])
                                tr = t.transpose(out=ps_kt2[:, fc * 128:(fc + 1) * 128],
                                                 in_=k_bfo[:, fc * 256 + tc * 128: fc * 256 + (tc + 1) * 128],
                                                 identity=identb[:])
                              pe_kt.inc(tr)
                            if KSCAN >= 2:
                              for h in range(NH):
                                kp = k_bfe if h % 2 == 0 else k_bfo
                                mm = t.matmul(out=ps_sT[:, h * 128:(h + 1) * 128],
                                              lhsT=kp[:, hcf(h) * 256 + tc * 128: hcf(h) * 256 + (tc + 1) * 128],
                                              rhs=q_bf[:, hcf(h) * 256 + tc * 128: hcf(h) * 256 + (tc + 1) * 128],
                                              start=True, stop=True)
                              pe_st.inc(mm)
                            if KSCAN >= 5:
                              t.wait_ge(dve_kt.h, kt_fut)
                              t.wait_ge(dve_state.h, state_now)
                              for h in range(NH):
                                mm = t.matmul(out=ps_sT[hb(h):hb(h) + 64, 1024 + hcf(h) * 64: 1024 + (hcf(h) + 1) * 64],
                                              lhsT=kt_bf[:, (h % 2) * 1024 + hcf(h) * 128 + (h % 2) * 64:
                                                         (h % 2) * 1024 + hcf(h) * 128 + (h % 2) * 64 + 64],
                                              rhs=vt_bf[:, tc * 1024 + h * 64: tc * 1024 + (h + 1) * 64],
                                              start=True, stop=True,
                                              tile_position=(0, hb(h)))
                              pe_kv.inc(mm)
                            if KSCAN >= 4:
                              t.wait_ge(dve_smt.h, smt_fut)
                              t.wait_ge(dve_state.h, state_now)
                              for h in range(NH):
                                t.matmul(out=ps_n[hb(h):hb(h) + 64, hcf(h) * 128:(hcf(h) + 1) * 128],
                                         lhsT=vt_bf[:, tc * 1024 + h * 64: tc * 1024 + (h + 1) * 64],
                                         rhs=smt_sb[:, h * 128:(h + 1) * 128],
                                         start=True, stop=False,
                                         tile_position=(0, hb(h)))
                                kvp = kv_bfe if h % 2 == 0 else kv_bfo
                                mm = t.matmul(out=ps_n[hb(h):hb(h) + 64, hcf(h) * 128:(hcf(h) + 1) * 128],
                                              lhsT=kvp[:, hcf(h) * 64:(hcf(h) + 1) * 64],
                                              rhs=q_bf[:, hcf(h) * 256 + tc * 128: hcf(h) * 256 + (tc + 1) * 128],
                                              start=False, stop=True,
                                              tile_position=(0, hb(h)))
                            pe_num.inc(mm)
                            if KSCAN >= 3:
                              if KSCAN < 4:
                                t.wait_ge(dve_smt.h, smt_fut)
                              for h in range(NH):
                                t.matmul(out=ps_d[hb(h):hb(h) + 64, hcf(h) * 128:(hcf(h) + 1) * 128],
                                         lhsT=onesc[:],
                                         rhs=smt_sb[:, h * 128:(h + 1) * 128],
                                         start=True, stop=False,
                                         tile_position=(0, hb(h)))
                                ksp = ksbce if h % 2 == 0 else ksbco
                                mm = t.matmul(out=ps_d[hb(h):hb(h) + 64, hcf(h) * 128:(hcf(h) + 1) * 128],
                                              lhsT=ksp[:, hcf(h) * 64:(hcf(h) + 1) * 64],
                                              rhs=q_bf[:, hcf(h) * 256 + tc * 128: hcf(h) * 256 + (tc + 1) * 128],
                                              start=False, stop=True,
                                              tile_position=(0, hb(h)))
                              pe_den.inc(mm)
                        block.tensor(_t4)

                        def _a4(a, cg=cg, rbc_sb=rbc_sb):
                            if KSCAN < 3:
                                return
                            a.wait_ge(pe_den.h, cg + 1)
                            a.activation(out=rbc_sb[:], in_=ps_d[:, 0:1024], func=AF.Ln, bias=eps6[:])
                            e = a.activation(out=rbc_sb[:], in_=rbc_sb[:], func=AF.Exp, scale=-1.0)
                            act_den.inc(e)
                        block.scalar(_a4)

                        def _v4(v, tc=tc, cg=cg, first_scan=first_scan,
                                k_bfe=k_bfe, k_bfo=k_bfo, kt_bf=kt_bf, smt_sb=smt_sb, attn_bf=attn_bf,
                                rbc_sb=rbc_sb):
                            if first_scan:
                                v.memset(kv_f[:], 0.0)
                                v.memset(kv_bfe[:], 0.0)
                                v.memset(kv_bfo[:], 0.0)
                                v.memset(ksum[:], 0.0)
                                v.memset(ksbce[:], 0.0)
                                ms = v.memset(ksbco[:], 0.0)
                                dve_state.inc(ms)
                            if KSCAN >= 2:
                                v.wait_ge(pe_st.h, cg + 1)
                                sm = v.tensor_tensor(
                                    out=smt_sb[:].rearrange("p (h t) -> p h t", t=128),
                                    in0=ps_sT[:].rearrange("p (h t) -> p h t", t=128),
                                    in1=maskT[:].unsqueeze(1).broadcast_to((128, 16, 128)),
                                    op=ALU.mult)
                                dve_smt.inc(sm)
                            if KSCAN >= 1:
                                v.wait_ge(pe_kt.h, cg + 1)
                                kc_ = v.tensor_copy(out=kt_bf[:, 0:1024], in_=ps_kt[:])
                                dve_kt.inc(kc_)
                                kc_ = v.tensor_copy(out=kt_bf[:, 1024:2048], in_=ps_kt2[:])
                                dve_kt.inc(kc_)
                            if KSCAN >= 5:
                                v.wait_ge(pe_kv.h, cg + 1)
                                v.reduce_sum(out=ksum_d[0:64, :],
                                             in_=k_bfe[0:64, :].rearrange("p (c t) -> p c t", t=256)
                                                 [:, :, tc * 128:(tc + 1) * 128],
                                             axis=AX.X)
                                v.reduce_sum(out=ksum_d[64:128, :],
                                             in_=k_bfo[64:128, :].rearrange("p (c t) -> p c t", t=256)
                                                 [:, :, tc * 128:(tc + 1) * 128],
                                             axis=AX.X)
                                v.tensor_add(out=kv_f[:], in0=ps_kv, in1=kv_f[:])
                                v.tensor_add(out=ksum[:], in0=ksum[:], in1=ksum_d[:])
                                v.tensor_copy(out=kv_bfe[0:64, :], in_=kv_f[0:64, :])
                                v.tensor_copy(out=kv_bfo[64:128, :], in_=kv_f[64:128, :])
                                for c in range(8):
                                    v.tensor_tensor(out=ksbce[0:64, c * 64:(c + 1) * 64],
                                                    in0=onesc[0:64, 0:64],
                                                    in1=ksum[0:64, c:c + 1].broadcast_to((64, 64)),
                                                    op=ALU.mult)
                                    u3 = v.tensor_tensor(out=ksbco[64:128, c * 64:(c + 1) * 64],
                                                         in0=onesc[64:128, 0:64],
                                                         in1=ksum[64:128, c:c + 1].broadcast_to((64, 64)),
                                                         op=ALU.mult)
                                dve_state.inc(u3, 4)
                            if KSCAN >= 4:
                                v.wait_ge(pe_num.h, cg + 1)
                                v.wait_ge(act_den.h, cg + 1)
                                at = v.tensor_tensor(
                                    out=attn_bf[:].rearrange("p (c t) -> p c t", t=256)
                                        [:, :, tc * 128:(tc + 1) * 128],
                                    in0=ps_n[:].rearrange("p (c t) -> p c t", t=128),
                                    in1=rbc_sb[:].rearrange("p (c t) -> p c t", t=128),
                                    op=ALU.mult)
                                dve_attn.inc(at)
                        block.vector(_v4)

                        if KDBG and q == 0 and tile == 0 and tc == 0:
                            _dsv = dve_state.val
                            def _gdbg(gp, _dsv=_dsv):
                                gp.wait_ge(dve_state.h, _dsv)
                                gp.dma_start(out=dbgb_d[:, 0:2048], in_=kt_bf[:]).then_inc(s_st[0].h, 16)
                                gp.dma_start(out=dbgb_d[:, 2048:2560], in_=kv_bfe[:]).then_inc(s_st[0].h, 16)
                                gp.dma_start(out=dbgb_d[:, 2560:3072], in_=kv_bfo[:]).then_inc(s_st[0].h, 16)
                                gp.dma_start(out=dbgb_d[:, 3072:5120], in_=smt_sb[:]).then_inc(s_st[0].h, 16)
                                gp.dma_start(out=dbgb_d[:, 5120:5632], in_=ksbce[:]).then_inc(s_st[0].h, 16)
                                gp.dma_start(out=dbgb_d[:, 5632:6144], in_=ksbco[:]).then_inc(s_st[0].h, 16)
                                gp.dma_start(out=dbgb_d[:, 6144:8192], in_=vt_bf[:]).then_inc(s_st[0].h, 16)
                                gp.dma_start(out=dbgf_d[:, 0:512], in_=kv_f[:]).then_inc(s_st[0].h, 16)
                                gp.dma_start(out=dbgf_d[:, 512:520], in_=ksum[:]).then_inc(s_st[0].h, 16)
                                for _ in range(9):
                                    s_st[0].val += 16
                            block.gpsimd(_gdbg)

                S4_.close()
                if KSTAGE < 5:
                    continue
                S5_ = ExitStack()
                ps_o = S5_.enter_context(nc.psum_tensor(f"pso{q}_{tile}", [128, 2048], F32))
                ps_stat = S5_.enter_context(nc.psum_tensor(f"psstat{q}_{tile}", [1, 512], F32))
                ps_bc = S5_.enter_context(nc.psum_tensor(f"psbc{q}_{tile}", [128, 512], F32))
                if True:

                    pe_o_base = pe_o.val
                    dve_r1_base = dve_r1.val
                    act_ln_base = act_ln.val
                    dve_ln_base = dve_ln.val
                    pe_ln_base = pe_ln.val

                    def _t5(t, pe_o_base=pe_o_base, dve_r1_base=dve_r1_base,
                            act_ln_base=act_ln_base, dve_ln_base=dve_ln_base,
                            woT=woT, attn_bf=attn_bf, q_bf=q_bf, sq_bf=sq_bf, rows=rows):
                        t.wait_ge(dve_smt.h, dve_smt.val)
                        t.wait_ge(dve_kt.h, dve_kt.val)
                        t.wait_ge(dve_attn.h, dve_attn.val)
                        t.wait_ge(dve_state.h, dve_state.val)
                        t.wait_ge(act_den.h, act_den.val)
                        t.wait_ge(wa.h, w_done[3])
                        for oc in range(8):
                            for kc in range(8):
                                mm = t.matmul(out=ps_o[:, oc * 256:(oc + 1) * 256],
                                              lhsT=woT[:, kc * D + oc * 128: kc * D + (oc + 1) * 128],
                                              rhs=attn_bf[:, kc * 256:(kc + 1) * 256],
                                              start=(kc == 0), stop=(kc == 7))
                            pe_o.inc(mm)
                        t.wait_ge(dve_r1.h, dve_r1_base + 9)
                        t.wait_ge(act_ln.h, act_ln_base + 1)
                        for fc in range(8):
                            t.matmul(out=ps_stat[0:1, 0:256], lhsT=onesc[:, 0:1],
                                     rhs=q_bf[:, fc * 256:(fc + 1) * 256],
                                     start=(fc == 0), stop=(fc == 7))
                        for fc in range(8):
                            mm = t.matmul(out=ps_stat[0:1, 256:512], lhsT=onesc[:, 0:1],
                                          rhs=sq_bf[:, fc * 256:(fc + 1) * 256],
                                          start=(fc == 0), stop=(fc == 7))
                        pe_ln.inc(mm)
                        t.wait_ge(act_ln.h, act_ln_base + 3)
                        t.wait_ge(dve_ln.h, dve_ln_base + 4)
                        t.matmul(out=ps_bc[:, 0:256], lhsT=onesrf[:], rhs=rows[0:1, 512:768],
                                 start=True, stop=True)
                        mm = t.matmul(out=ps_bc[:, 256:512], lhsT=onesrf[:], rhs=rows[0:1, 768:1024],
                                      start=True, stop=True)
                        pe_ln.inc(mm)
                    block.tensor(_t5)

                    def _a5(a, dve_r1_base=dve_r1_base, dve_ln_base=dve_ln_base,
                            sq_bf=sq_bf, rows=rows, y1reg=y1reg):
                        a.wait_ge(dve_r1.h, dve_r1_base + 9)
                        s = a.activation(out=sq_bf[:].rearrange("p (c t) -> p c t", t=256),
                                         in_=y1reg, func=AF.Square)
                        act_ln.inc(s)
                        a.wait_ge(dve_ln.h, dve_ln_base + 3)
                        l = a.activation(out=rows[0:1, 256:512], in_=rows[0:1, 0:256],
                                         func=AF.Ln, bias=eps5[0:1, :])
                        act_ln.inc(l)
                        r = a.activation(out=rows[0:1, 512:768], in_=rows[0:1, 256:512],
                                         func=AF.Exp, scale=-0.5)
                        act_ln.inc(r)
                    block.scalar(_a5)

                    def _v5(v, tok0=tok0, pe_o_base=pe_o_base, pe_ln_base=pe_ln_base,
                            act_ln_base=act_ln_base,
                            q_bf=q_bf, rows=rows, x_fm=x_fm, y1reg=y1reg, y1breg=y1breg):
                        for oc in range(8):
                            v.wait_ge(pe_o.h, pe_o_base + (oc | 1) + 1)
                            r = v.scalar_tensor_tensor(
                                out=y1q[:, oc * QTOK + tok0: oc * QTOK + tok0 + TILE],
                                in0=ps_o[:, oc * 256:(oc + 1) * 256],
                                scalar=bo_col(oc), in1=x_fm[:, oc * 256:(oc + 1) * 256],
                                op0=ALU.add, op1=ALU.add)
                            dve_r1.inc(r)
                        rb = v.tensor_copy(out=q_bf[:].rearrange("p (c t) -> p c t", t=256),
                                           in_=y1reg)
                        dve_r1.inc(rb)
                        v.wait_ge(pe_ln.h, pe_ln_base + 1)
                        mu = v.tensor_scalar(out=rows[0:1, 1024:1280], in0=ps_stat[0:1, 0:256],
                                             scalar1=1.0 / D, scalar2=None, op0=ALU.mult)
                        dve_ln.inc(mu)
                        m2 = v.tensor_mul(out=rows[0:1, 1280:1536], in0=rows[0:1, 1024:1280],
                                          in1=rows[0:1, 1024:1280])
                        dve_ln.inc(m2)
                        var = v.scalar_tensor_tensor(out=rows[0:1, 0:256],
                                                     in0=ps_stat[0:1, 256:512],
                                                     scalar=1.0 / D, in1=rows[0:1, 1280:1536],
                                                     op0=ALU.mult, op1=ALU.subtract)
                        dve_ln.inc(var)
                        v.wait_ge(act_ln.h, act_ln_base + 3)
                        p = v.tensor_mul(out=rows[0:1, 768:1024], in0=rows[0:1, 1024:1280],
                                         in1=rows[0:1, 512:768])
                        dve_ln.inc(p)
                        v.wait_ge(pe_ln.h, pe_ln_base + 2)
                        n1 = v.tensor_tensor(out=y1reg, in0=y1reg,
                                             in1=ps_bc[:, 0:256].unsqueeze(1)
                                                 .broadcast_to((128, 8, 256)),
                                             op=ALU.mult)
                        dve_ln.inc(n1)
                        n2 = v.tensor_tensor(out=y1reg, in0=y1reg,
                                             in1=ps_bc[:, 256:512].unsqueeze(1)
                                                 .broadcast_to((128, 8, 256)),
                                             op=ALU.subtract)
                        dve_ln.inc(n2)
                        yb = v.tensor_copy(out=y1breg, in_=y1reg)
                        dve_ln.inc(yb)
                    block.vector(_v5)
                S5_.close()

            barrier()
            A_.close()

        if not EMIT_MOE:
            with nc.sbuf_tensor(f"douttm{q}", [128, 2048], F32) as out_tm_dbg, \
                 nc.psum_tensor(f"dpsot{q}", [128, 1024], F32) as ps_ot_dbg:
                for tile in range(NTILE):
                    tok0 = tile * TILE
                    atok = qbase + tok0
                    for tc in range(2):
                        k = tile * 2 + tc
                        def _tD(t, tc=tc, tok0=tok0, dve_ot_now=dve_ot.val):
                            t.wait_ge(dve_ot.h, dve_ot_now)
                            for fc in range(8):
                                tr = t.transpose(
                                    out=ps_ot_dbg[:, fc * 128:(fc + 1) * 128],
                                    in_=y1q[:, fc * QTOK + tok0 + tc * 128:
                                            fc * QTOK + tok0 + (tc + 1) * 128],
                                    identity=identf[:])
                            pe_ot.inc(tr)
                        block.tensor(_tD)
                        def _vD2(v, k=k, pe_ot_now=pe_ot.val):
                            v.wait_ge(pe_ot.h, pe_ot_now)
                            sem = s_st[k % 2]
                            v.wait_ge(sem.h, sem.val)
                            ev = v.tensor_copy(out=out_tm_dbg[:, (k % 2) * 1024:(k % 2 + 1) * 1024],
                                               in_=ps_ot_dbg[:])
                            dve_ot.inc(ev)
                        block.vector(_vD2)
                        def _gD(gp, k=k, atok=atok, tc=tc, dve_ot_now=dve_ot.val):
                            gp.wait_ge(dve_ot.h, dve_ot_now)
                            sem = s_st[k % 2]
                            gp.dma_start(out=out_d[seq, atok + tc * 128: atok + (tc + 1) * 128, :],
                                         in_=out_tm_dbg[:, (k % 2) * 1024:(k % 2 + 1) * 1024]
                                         ).then_inc(sem.h, 16)
                            sem.val += 16
                        block.gpsimd(_gD)
            barrier()
            continue

        R_ = ExitStack()
        ps_rt = R_.enter_context(nc.psum_tensor(f"psrt{q}", [32, 256], F32))
        ps_rtt = R_.enter_context(nc.psum_tensor(f"psrtt{q}", [128, 1024], F32))
        sc_em = R_.enter_context(nc.sbuf_tensor(f"rt_sc{q}", [128, 256], F32))
        sco_em = R_.enter_context(nc.sbuf_tensor(f"rt_sco{q}", [128, 256], F32))
        rt_tm = R_.enter_context(nc.sbuf_tensor(f"rt_tm{q}", [128, 384], F32))
        rt_wk = R_.enter_context(nc.sbuf_tensor(f"rt_wk{q}", [128, 96], F32))
        rt_cols = R_.enter_context(nc.sbuf_tensor(f"rt_cols{q}", [128, 16], F32))
        if True:
            for tile in range(NTILE):
                tok0 = tile * TILE
                pe_rt_b = pe_rt.val
                act_rt_b = act_rt.val
                dve_rt_b = dve_rt.val

                def _t6(t, tok0=tok0, first=(tile == 0), act_rt_now=act_rt.val):
                    if not first:
                        t.wait_ge(act_rt.h, act_rt_now)
                    for kc in range(8):
                        mm = t.matmul(out=ps_rt[:], lhsT=gwT[:, kc * 32:(kc + 1) * 32],
                                      rhs=y1q[:, kc * QTOK + tok0: kc * QTOK + tok0 + TILE],
                                      start=(kc == 0), stop=(kc == 7))
                    pe_rt.inc(mm)
                block.tensor(_t6)

                def _a6(a, pe_rt_b=pe_rt_b, sco_em=sco_em):
                    a.wait_ge(pe_rt.h, pe_rt_b + 1)
                    s = a.activation(out=sco_em[0:32, :], in_=ps_rt[:], func=AF.Sigmoid)
                    act_rt.inc(s)
                block.scalar(_a6)

                def _v6(v, act_rt_b=act_rt_b, sc_em=sc_em, sco_em=sco_em):
                    v.wait_ge(act_rt.h, act_rt_b + 1)
                    if tile == 0:
                        v.memset(rt_tm[:], 0.0)
                        v.memset(rt_wk[:], 0.0)
                        v.memset(rt_cols[:], 0.0)
                        v.memset(sc_em[32:64, :], 0.0)
                        v.memset(sc_em[64:128, :], 0.0)
                        v.memset(sco_em[32:64, :], 0.0)
                        v.memset(sco_em[64:128, :], 0.0)
                    s = v.tensor_scalar(out=sc_em[0:32, :], in0=sco_em[0:32, :], scalar1=gb_col[:],
                                        scalar2=None, op0=ALU.add)
                    dve_rt.inc(s)
                block.vector(_v6)

                def _t7(t, dve_rt_now=dve_rt.val, sc_em=sc_em, sco_em=sco_em):
                    t.wait_ge(dve_rt.h, dve_rt_now)
                    for tc in range(2):
                        t.transpose(out=ps_rtt[:, tc * 256: tc * 256 + 128],
                                    in_=sc_em[:, tc * 128:(tc + 1) * 128],
                                    identity=identf[:])
                        tr = t.transpose(out=ps_rtt[:, tc * 256 + 128: tc * 256 + 256],
                                         in_=sco_em[:, tc * 128:(tc + 1) * 128],
                                         identity=identf[:])
                    pe_rt.inc(tr)
                block.tensor(_t7)

                def _v7(v, pe_rt_now=pe_rt.val):
                    TCS = (0, 1)
                    F = lambda tc, a, b: rt_tm[:, tc * 192 + a: tc * 192 + b]
                    W = lambda tc, a, b: rt_wk[:, tc * 48 + a: tc * 48 + b]
                    C = lambda tc, i: rt_cols[:, tc * 8 + i: tc * 8 + i + 1]
                    bc = lambda col, n: col.broadcast_to((128, n))
                    v.wait_ge(pe_rt.h, pe_rt_now)
                    for tc in TCS:
                        v.tensor_copy(out=F(tc, 0, 32), in_=ps_rtt[:, tc * 256: tc * 256 + 32])
                    for tc in TCS:
                        v.tensor_copy(out=F(tc, 32, 64),
                                      in_=ps_rtt[:, tc * 256 + 128: tc * 256 + 160])
                    for g in range(4):
                        for tc in TCS:
                            v.reduce_max(out=C(tc, 0), in_=F(tc, 8 * g, 8 * g + 8), axis=AX.X)
                        for tc in TCS:
                            v.tensor_tensor(out=W(tc, 0, 8), in0=F(tc, 8 * g, 8 * g + 8),
                                            in1=bc(C(tc, 0), 8), op=ALU.is_equal)
                        for tc in TCS:
                            v.scalar_tensor_tensor(out=W(tc, 0, 8), in0=W(tc, 0, 8),
                                                   scalar=-1e30, in1=F(tc, 8 * g, 8 * g + 8),
                                                   op0=ALU.mult, op1=ALU.add)
                        for tc in TCS:
                            v.reduce_max(out=C(tc, 1), in_=W(tc, 0, 8), axis=AX.X)
                        for tc in TCS:
                            v.tensor_add(out=F(tc, 176, 177)[:, 0:1] if False else
                                         rt_tm[:, tc * 192 + 176 + g: tc * 192 + 177 + g],
                                         in0=C(tc, 0), in1=C(tc, 1))
                    for tc in TCS:
                        v.reduce_max(out=C(tc, 0), in_=F(tc, 176, 180), axis=AX.X)
                    for tc in TCS:
                        v.tensor_tensor(out=W(tc, 8, 12), in0=F(tc, 176, 180),
                                        in1=bc(C(tc, 0), 4), op=ALU.is_equal)
                    for tc in TCS:
                        v.scalar_tensor_tensor(out=W(tc, 8, 12), in0=W(tc, 8, 12),
                                               scalar=-1e30, in1=F(tc, 176, 180),
                                               op0=ALU.mult, op1=ALU.add)
                    for tc in TCS:
                        v.reduce_max(out=C(tc, 1), in_=W(tc, 8, 12), axis=AX.X)
                    for tc in TCS:
                        v.tensor_tensor(out=W(tc, 8, 12), in0=F(tc, 176, 180),
                                        in1=bc(C(tc, 1), 4), op=ALU.is_ge)
                    for tc in TCS:
                        v.tensor_scalar(out=W(tc, 8, 12), in0=W(tc, 8, 12), scalar1=1.0,
                                        scalar2=1e30, op0=ALU.subtract, op1=ALU.mult)
                    for g in range(4):
                        for tc in TCS:
                            v.tensor_tensor(out=F(tc, 64 + 8 * g, 72 + 8 * g),
                                            in0=F(tc, 8 * g, 8 * g + 8),
                                            in1=bc(rt_wk[:, tc * 48 + 8 + g: tc * 48 + 9 + g], 8),
                                            op=ALU.add)
                    for tc in TCS:
                        v.tensor_copy(out=F(tc, 96, 128), in_=F(tc, 64, 96))
                    for tc in TCS:
                        v.reduce_max(out=C(tc, 2), in_=F(tc, 96, 128), axis=AX.X)
                    for it in range(5):
                        for tc in TCS:
                            src = C(tc, 2) if it == 0 else C(tc, 1)
                            v.tensor_tensor(out=W(tc, 16, 48), in0=F(tc, 96, 128),
                                            in1=bc(src, 32), op=ALU.is_equal)
                        for tc in TCS:
                            v.scalar_tensor_tensor(out=F(tc, 96, 128), in0=W(tc, 16, 48),
                                                   scalar=-1e30, in1=F(tc, 96, 128),
                                                   op0=ALU.mult, op1=ALU.add)
                        if it < 4:
                            for tc in TCS:
                                v.reduce_max(out=C(tc, 1), in_=F(tc, 96, 128), axis=AX.X)
                    for tc in TCS:
                        v.reduce_max(out=C(tc, 3), in_=F(tc, 96, 128), axis=AX.X)
                    for tc in TCS:
                        v.tensor_tensor(out=W(tc, 16, 48), in0=F(tc, 64, 96),
                                        in1=bc(C(tc, 3), 32), op=ALU.is_ge)
                    for tc in TCS:
                        v.tensor_mul(out=W(tc, 16, 48), in0=F(tc, 32, 64), in1=W(tc, 16, 48))
                    for tc in TCS:
                        v.reduce_sum(out=C(tc, 4), in_=W(tc, 16, 48), axis=AX.X)
                    for tc in TCS:
                        v.tensor_scalar(out=C(tc, 4), in0=C(tc, 4), scalar1=1e-20,
                                        scalar2=None, op0=ALU.add)
                    for tc in TCS:
                        v.reciprocal(out=C(tc, 4), in_=C(tc, 4))
                    for tc in TCS:
                        v.tensor_scalar(out=C(tc, 4), in0=C(tc, 4), scalar1=0.2,
                                        scalar2=None, op0=ALU.mult)
                    for tc in TCS:
                        v.tensor_tensor(out=F(tc, 128, 160), in0=W(tc, 16, 48),
                                        in1=bc(C(tc, 4), 32), op=ALU.mult)
                    for tc in TCS:
                        v.tensor_tensor(out=W(tc, 16, 48), in0=F(tc, 64, 96),
                                        in1=bc(C(tc, 2), 32), op=ALU.is_ge)
                    last = None
                    for tc in TCS:
                        w2 = W(tc, 16, 48).rearrange("p (l two) -> p l two", two=2)
                        last = v.tensor_add(out=F(tc, 160, 176), in0=w2[:, :, 0:1].squeeze(2),
                                            in1=w2[:, :, 1:2].squeeze(2))
                    dve_rt.inc(last)
                block.vector(_v7)

                def _t8(t, dve_rt_now=dve_rt.val):
                    t.wait_ge(dve_rt.h, dve_rt_now)
                    for tc in range(2):
                        t.transpose(out=ps_rtt[0:32, 512 + tc * 256: 512 + tc * 256 + 128],
                                    in_=rt_tm[:, tc * 192 + 128: tc * 192 + 160],
                                    identity=identf[:])
                        tr = t.transpose(out=ps_rtt[0:16, 512 + tc * 256 + 128:
                                                    512 + tc * 256 + 256],
                                         in_=rt_tm[:, tc * 192 + 160: tc * 192 + 176],
                                         identity=identf[:])
                    pe_rt.inc(tr)
                block.tensor(_t8)

                def _v8(v, tile=tile, pe_rt_now=pe_rt.val):
                    v.wait_ge(pe_rt.h, pe_rt_now)
                    last = None
                    for tc in range(2):
                        v.tensor_copy(out=cw_em[:, tile * 256 + tc * 128:
                                                tile * 256 + (tc + 1) * 128],
                                      in_=ps_rtt[0:32, 512 + tc * 256: 512 + tc * 256 + 128])
                        last = v.tensor_copy(out=acw_em[:, tile * 256 + tc * 128:
                                                        tile * 256 + (tc + 1) * 128],
                                             in_=ps_rtt[0:16, 512 + tc * 256 + 128:
                                                        512 + tc * 256 + 256])
                    dve_rt.inc(last)
                block.vector(_v8)
        R_.close()
        if KDBG and q == 0:
            _rtv = dve_rt.val
            def _gdbg2(gp, _rtv=_rtv):
                gp.wait_ge(dve_rt.h, _rtv)
                gp.dma_start(out=dbgb2_d[0:32, 0:1024], in_=cw_em[:]).then_inc(s_st[1].h, 16)
                gp.dma_start(out=dbgb2_d[0:16, 1024:2048], in_=acw_em[:]).then_inc(s_st[1].h, 16)
                s_st[1].val += 32
            block.gpsimd(_gdbg2)
        barrier()

        B_ = ExitStack()
        _bsb = lambda n, sh, dt: B_.enter_context(nc.sbuf_tensor(n, sh, dt))
        wgb0 = _bsb(f"wgb{q}_0", [128, 4096], BF16)
        wgb1 = _bsb(f"wgb{q}_1", [128, 4096], BF16)
        wgb2 = _bsb(f"wgb{q}_2", [128, 4096], BF16)
        wgb3 = _bsb(f"wgb{q}_3", [128, 4096], BF16)
        wub0 = _bsb(f"wub{q}_0", [128, 4096], BF16)
        wub1 = _bsb(f"wub{q}_1", [128, 4096], BF16)
        wub2 = _bsb(f"wub{q}_2", [128, 4096], BF16)
        wub3 = _bsb(f"wub{q}_3", [128, 4096], BF16)
        wdb0 = _bsb(f"wdb{q}_0", [128, 4096], BF16)
        wdb1 = _bsb(f"wdb{q}_1", [128, 4096], BF16)
        wdb2 = _bsb(f"wdb{q}_2", [128, 4096], BF16)
        wdb3 = _bsb(f"wdb{q}_3", [128, 4096], BF16)
        cwb_sb = _bsb(f"cwbs{q}", [128, 2048], BF16)
        silu_sb = _bsb(f"silu{q}", [128, 2048], F32)
        tmp_f = _bsb(f"tmpf{q}", [128, 1024], F32)
        hpr_sb = _bsb(f"hpr{q}", [128, 4096], BF16)
        oh32 = _bsb(f"oh32{q}", [32, 32 * 128], BF16)
        oh16 = _bsb(f"oh16{q}", [16, 16 * 128], BF16)
        ps_g = B_.enter_context(nc.psum_tensor(f"psg{q}", [128, 1024], F32))
        ps_u = B_.enter_context(nc.psum_tensor(f"psu{q}", [128, 1024], F32))
        ps_down = B_.enter_context(nc.psum_tensor(f"psdn{q}", [128, 2048], F32))
        if True:
            wgb = [wgb0, wgb1, wgb2, wgb3]
            wub = [wub0, wub1, wub2, wub3]
            wdb = [wdb0, wdb1, wdb2, wdb3]

            def emit_loads(slist, q=q):
                def _sl(sync, slist=tuple(slist), q=q):
                    for s_ in slist:
                        if q == 0:
                            csem, cval = cast_grp[s_]
                            sync.wait_ge(csem.h, cval)
                        kind, idx = subs[s_]
                        sg, su, sdn, cg, cu, cd, n_i = sub_srcs(kind, idx)
                        b = s_ % 4
                        if s_ >= 4:
                            sync.wait_ge(pe_down.h, down_after[s_ - 4])
                        sem = s_wb[b]
                        sync.dma_start(out=wgb[b][:, 0:8 * n_i], in_=cg).then_inc(sem.h, 16)
                        sem.val += 16
                        sync.dma_start(out=wub[b][:, 0:8 * n_i], in_=cu).then_inc(sem.h, 16)
                        sem.val += 16
                        sync.dma_start(out=wdb[b][:, 0:(n_i // 128) * 1024],
                                       in_=cd).then_inc(sem.h, 16)
                        sem.val += 16
                        wval[s_] = (sem, sem.val)
                block.sync(_sl)

            def _soh(sync):
                sync.dma_start(out=oh32[:], in_=oh32_d).then_inc(s_oh.h, 16)
                sync.dma_start(out=oh16[:], in_=oh16_d).then_inc(s_oh.h, 16)
                s_oh.val += 32
            block.sync(_soh)
            emit_loads([0, 1, 2, 3])
            next_load = 4

            for p, (s0, nsub) in enumerate(pair_of[:N_PAIR_EMIT]):
                ss = list(range(s0, s0 + nsub))
                has_cw = subs[s0][0] != 1
                act_b = act_silu.val
                dh_b = dve_h.val
                ev_b = dve_evac.val
                gu_b = pe_gu.val
                cwb_tgt = None

                def _tp(t, p=p, ss=tuple(ss), nsub=nsub, has_cw=has_cw,
                        act_b=act_b, dh_b=dh_b, ev_b=ev_b, cw_b=dve_cwb.val):
                    for s_ in ss:
                        sem, v_ = wval[s_]
                        t.wait_ge(sem.h, v_)
                    t.wait_ge(dve_evac.h, ev_b)
                    if has_cw:
                        t.wait_ge(dve_rt.h, dve_rt.val)
                        t.wait_ge(s_oh.h, s_oh.val)
                        for j, s_ in enumerate(ss):
                            kind, idx = subs[s_]
                            oh = oh32 if kind == 0 else oh16
                            src = cw_em if kind == 0 else acw_em
                            for tile in range(NTILE):
                                slot = j * 4 + tile
                                pst = ps_g if slot < 4 else ps_u
                                mm = t.matmul(out=pst[:, (slot % 4) * 256:(slot % 4 + 1) * 256],
                                              lhsT=oh[:, idx * 128:(idx + 1) * 128],
                                              rhs=src[:, tile * 256:(tile + 1) * 256],
                                              start=True, stop=True)
                        pe_cwb.inc(mm)

                    def downs(tile):
                        t.wait_ge(dve_h.h, dh_b + tile * nsub + nsub)
                        t.wait_ge(dve_evac.h, ev_b + tile)
                        if has_cw and tile == 0:
                            t.wait_ge(dve_cwb.h, cw_b + 2)
                        for oc in range(8):
                            for j, s_ in enumerate(ss):
                                kind, idx = subs[s_]
                                n_ic = (MOE_I if kind != 2 else ANC_I) // 128
                                b = s_ % 4
                                for ic in range(n_ic):
                                    mm = t.matmul(out=ps_down[:, oc * 256:(oc + 1) * 256],
                                                  lhsT=wdb[b][:, ic * 1024 + oc * 128:
                                                              ic * 1024 + (oc + 1) * 128],
                                                  rhs=hpr_sb[:, (tile % 2) * 2048 + j * 1024
                                                             + ic * 256:
                                                             (tile % 2) * 2048 + j * 1024
                                                             + (ic + 1) * 256],
                                                  start=(j == 0 and ic == 0),
                                                  stop=(j == nsub - 1 and ic == n_ic - 1))
                        pe_down.inc(mm, nsub)

                    for tile in range(NTILE):
                        tok0 = tile * TILE
                        for j, s_ in enumerate(ss):
                            oi = tile * nsub + j
                            kind, idx = subs[s_]
                            n_i = MOE_I if kind != 2 else ANC_I
                            n_ich = n_i // 128
                            b = s_ % 4
                            t.wait_ge(act_silu.h, act_b + oi)
                            if has_cw and oi == 0:
                                t.wait_ge(dve_cwb.h, cw_b + 2)
                            for ich in range(n_ich):
                                for kc in range(8):
                                    gmm = t.matmul(out=ps_g[:, ich * 256:(ich + 1) * 256],
                                                   lhsT=wgb[b][:, kc * n_i + ich * 128:
                                                               kc * n_i + (ich + 1) * 128],
                                                   rhs=y1qbf[:, kc * QTOK + tok0:
                                                             kc * QTOK + tok0 + TILE],
                                                   start=(kc == 0), stop=(kc == 7))
                            pe_g.inc(gmm)
                            t.wait_ge(dve_h.h, dh_b + oi)
                            for ich in range(n_ich):
                                for kc in range(8):
                                    mm = t.matmul(out=ps_u[:, ich * 256:(ich + 1) * 256],
                                                  lhsT=wub[b][:, kc * n_i + ich * 128:
                                                              kc * n_i + (ich + 1) * 128],
                                                  rhs=y1qbf[:, kc * QTOK + tok0:
                                                            kc * QTOK + tok0 + TILE],
                                                  start=(kc == 0), stop=(kc == 7))
                            pe_gu.inc(mm)
                        if tile >= 1:
                            downs(tile - 1)
                    downs(NTILE - 1)
                    for s_ in ss:
                        down_after[s_] = pe_down.val
                block.tensor(_tp)
                for s_ in ss:
                    down_after[s_] = pe_down.val


                def _ap(a, ss=tuple(ss), nsub=nsub, act_b=act_b, dh_b=dh_b, gu_b=gu_b,
                        pg_b=pe_g.val - 4 * nsub, has_cw=has_cw, cwb_now=pe_cwb.val):
                    if has_cw:
                        a.wait_ge(pe_cwb.h, cwb_now)
                        cc = a.activation(out=cwb_sb[:, 0:1024], in_=ps_g[:], func=AF.Identity)
                        dve_cwb.inc(cc, 1)
                    for tile in range(NTILE):
                        for j, s_ in enumerate(ss):
                            oi = tile * nsub + j
                            kind, idx = subs[s_]
                            n_i = MOE_I if kind != 2 else ANC_I
                            n_ich = n_i // 128
                            a.wait_ge(pe_g.h, pg_b + oi + 1)
                            if oi >= 2:
                                a.wait_ge(dve_h.h, dh_b + oi - 1)
                            sl = a.activation(out=silu_sb[:, (oi % 2) * 1024:
                                                          (oi % 2) * 1024 + n_ich * 256],
                                              in_=ps_g[:, 0:n_ich * 256], func=AF.Silu)
                            act_silu.inc(sl)
                block.scalar(_ap)

                if p + 1 < len(pair_of):
                    s0n, nsn = pair_of[p + 1]
                    want = list(range(next_load, s0n + nsn))
                    if want:
                        emit_loads(want)
                        next_load = want[-1] + 1

                pe_down_b = pe_down.val - 4 * nsub
                def _vp(v, p=p, ss=tuple(ss), nsub=nsub, has_cw=has_cw, act_b=act_b,
                        dh_b=dh_b, gu_b=gu_b, pe_down_b=pe_down_b, cwb_now=pe_cwb.val):
                    if has_cw:
                        v.wait_ge(pe_cwb.h, cwb_now)
                        cc = v.tensor_copy(out=cwb_sb[:, 1024:2048], in_=ps_u[:])
                        dve_cwb.inc(cc, 1)
                    for tile in range(NTILE):
                        tok0 = tile * TILE
                        for j, s_ in enumerate(ss):
                            oi = tile * nsub + j
                            kind, idx = subs[s_]
                            n_i = MOE_I if kind != 2 else ANC_I
                            n_ich = n_i // 128
                            v.wait_ge(pe_gu.h, gu_b + oi + 1)
                            v.wait_ge(act_silu.h, act_b + oi + 1)
                            if tile >= 2:
                                v.wait_ge(pe_down.h, pe_down_b + (tile - 1) * nsub)
                            hbase = (tile % 2) * 2048 + j * 1024
                            v.tensor_mul(out=tmp_f[:, 0:n_ich * 256],
                                         in0=silu_sb[:, (oi % 2) * 1024:
                                                     (oi % 2) * 1024 + n_ich * 256],
                                         in1=ps_u[:, 0:n_ich * 256])
                            if kind != 1:
                                slot = j * 4 + tile
                                hh = v.tensor_tensor(
                                    out=hpr_sb[:, hbase:hbase + n_ich * 256]
                                        .rearrange("p (c t) -> p c t", t=256),
                                    in0=tmp_f[:, 0:n_ich * 256]
                                        .rearrange("p (c t) -> p c t", t=256),
                                    in1=cwb_sb[:, slot * 256:(slot + 1) * 256]
                                        .unsqueeze(1).broadcast_to((128, n_ich, 256)),
                                    op=ALU.mult)
                            else:
                                hh = v.tensor_copy(
                                    out=hpr_sb[:, hbase:hbase + n_ich * 256],
                                    in_=tmp_f[:, 0:n_ich * 256])
                            dve_h.inc(hh)
                        if tile >= 1:
                            et = tile - 1
                            etok = et * TILE
                            v.wait_ge(pe_down.h, pe_down_b + (et + 1) * nsub)
                            ev = v.tensor_tensor(
                                out=y1q[:].rearrange("p (c t) -> p c t", t=QTOK)
                                    [:, :, etok:etok + TILE],
                                in0=ps_down[:].rearrange("p (c t) -> p c t", t=256),
                                in1=y1q[:].rearrange("p (c t) -> p c t", t=QTOK)
                                    [:, :, etok:etok + TILE],
                                op=ALU.add)
                            dve_evac.inc(ev)
                    et = NTILE - 1
                    etok = et * TILE
                    v.wait_ge(pe_down.h, pe_down_b + (et + 1) * nsub)
                    ev = v.tensor_tensor(
                        out=y1q[:].rearrange("p (c t) -> p c t", t=QTOK)
                            [:, :, etok:etok + TILE],
                        in0=ps_down[:].rearrange("p (c t) -> p c t", t=256),
                        in1=y1q[:].rearrange("p (c t) -> p c t", t=QTOK)
                            [:, :, etok:etok + TILE],
                        op=ALU.add)
                    dve_evac.inc(ev)
                block.vector(_vp)

            barrier()
            B_.close()

        L_ = ExitStack()
        r2bf = L_.enter_context(nc.sbuf_tensor(f"r2bf{q}", [128, 2048], BF16))
        sq2 = L_.enter_context(nc.sbuf_tensor(f"sq2{q}", [128, 2048], BF16))
        rows2 = L_.enter_context(nc.sbuf_tensor(f"rows2{q}", [1, 1536], F32))
        out_tm = L_.enter_context(nc.sbuf_tensor(f"outtm{q}", [128, 2048], F32))
        ps_st2 = L_.enter_context(nc.psum_tensor(f"psst2{q}", [1, 512], F32))
        ps_bc2 = L_.enter_context(nc.psum_tensor(f"psbc2{q}", [128, 512], F32))
        ps_ot = L_.enter_context(nc.psum_tensor(f"psot{q}", [128, 1024], F32))
        if True:
            for tile in range(NTILE):
                tok0 = tile * TILE
                atok = qbase + tok0
                y1reg = y1q[:].rearrange("p (c t) -> p c t", t=QTOK)[:, :, tok0:tok0 + TILE]
                dve_ln_b = dve_ln.val
                act_ln_b = act_ln.val
                pe_ln_b = pe_ln.val

                def _vA(v, y1reg=y1reg):
                    ra = v.tensor_copy(out=r2bf[:].rearrange("p (c t) -> p c t", t=256),
                                       in_=y1reg)
                    dve_ln.inc(ra)
                block.vector(_vA)

                def _aA(a, y1reg=y1reg, dve_ln_b=dve_ln_b):
                    a.wait_ge(dve_ln.h, dve_ln_b + 1)
                    sq = a.activation(out=sq2[:].rearrange("p (c t) -> p c t", t=256),
                                      in_=y1reg, func=AF.Square)
                    act_ln.inc(sq)
                block.scalar(_aA)

                def _tA(t, tile=tile, dve_ln_b=dve_ln_b, act_ln_b=act_ln_b):
                    t.wait_ge(dve_ln.h, dve_ln_b + 1)
                    t.wait_ge(act_ln.h, act_ln_b + 1)
                    if tile > 0:
                        t.wait_ge(dve_ln.h, dve_ln_b)
                    for fc in range(8):
                        t.matmul(out=ps_st2[0:1, 0:256], lhsT=onesc[:, 0:1],
                                 rhs=r2bf[:, fc * 256:(fc + 1) * 256],
                                 start=(fc == 0), stop=(fc == 7))
                    for fc in range(8):
                        mm = t.matmul(out=ps_st2[0:1, 256:512], lhsT=onesc[:, 0:1],
                                      rhs=sq2[:, fc * 256:(fc + 1) * 256],
                                      start=(fc == 0), stop=(fc == 7))
                    pe_ln.inc(mm)
                block.tensor(_tA)

                def _vB(v, pe_ln_b=pe_ln_b, act_ln_b=act_ln_b):
                    v.wait_ge(pe_ln.h, pe_ln_b + 1)
                    mu = v.tensor_scalar(out=rows2[0:1, 1024:1280], in0=ps_st2[0:1, 0:256],
                                         scalar1=1.0 / D, scalar2=None, op0=ALU.mult)
                    dve_ln.inc(mu)
                    m2 = v.tensor_mul(out=rows2[0:1, 1280:1536], in0=rows2[0:1, 1024:1280],
                                      in1=rows2[0:1, 1024:1280])
                    dve_ln.inc(m2)
                    var = v.scalar_tensor_tensor(out=rows2[0:1, 0:256],
                                                 in0=ps_st2[0:1, 256:512],
                                                 scalar=1.0 / D, in1=rows2[0:1, 1280:1536],
                                                 op0=ALU.mult, op1=ALU.subtract)
                    dve_ln.inc(var)
                block.vector(_vB)

                def _aB(a, dve_ln_b=dve_ln_b):
                    a.wait_ge(dve_ln.h, dve_ln_b + 4)
                    a.activation(out=rows2[0:1, 256:512], in_=rows2[0:1, 0:256],
                                 func=AF.Ln, bias=eps5[0:1, :])
                    r = a.activation(out=rows2[0:1, 512:768], in_=rows2[0:1, 256:512],
                                     func=AF.Exp, scale=-0.5)
                    act_ln.inc(r)
                block.scalar(_aB)

                def _vC(v, act_ln_b=act_ln_b):
                    v.wait_ge(act_ln.h, act_ln_b + 2)
                    p_ = v.tensor_mul(out=rows2[0:1, 768:1024], in0=rows2[0:1, 1024:1280],
                                      in1=rows2[0:1, 512:768])
                    dve_ln.inc(p_)
                block.vector(_vC)

                def _tB(t, dve_ln_b=dve_ln_b):
                    t.wait_ge(dve_ln.h, dve_ln_b + 5)
                    t.matmul(out=ps_bc2[:, 0:256], lhsT=onesrf[:], rhs=rows2[0:1, 512:768],
                             start=True, stop=True)
                    mm = t.matmul(out=ps_bc2[:, 256:512], lhsT=onesrf[:],
                                  rhs=rows2[0:1, 768:1024], start=True, stop=True)
                    pe_ln.inc(mm)
                block.tensor(_tB)

                def _vD(v, y1reg=y1reg, pe_ln_b=pe_ln_b):
                    v.wait_ge(pe_ln.h, pe_ln_b + 2)
                    n1 = v.tensor_tensor(out=y1reg, in0=y1reg,
                                         in1=ps_bc2[:, 0:256].unsqueeze(1)
                                             .broadcast_to((128, 8, 256)),
                                         op=ALU.mult)
                    dve_ln.inc(n1)
                    n2 = v.tensor_tensor(out=y1reg, in0=y1reg,
                                         in1=ps_bc2[:, 256:512].unsqueeze(1)
                                             .broadcast_to((128, 8, 256)),
                                         op=ALU.subtract)
                    dve_ln.inc(n2)
                block.vector(_vD)

                for tc in range(2):
                    k = tile * 2 + tc
                    def _tC(t, tc=tc, tok0=tok0, dve_ln_now=dve_ln.val,
                            dve_ot_now=dve_ot.val):
                        t.wait_ge(dve_ln.h, dve_ln_now)
                        t.wait_ge(dve_ot.h, dve_ot_now)
                        for fc in range(8):
                            tr = t.transpose(
                                out=ps_ot[:, fc * 128:(fc + 1) * 128],
                                in_=y1q[:, fc * QTOK + tok0 + tc * 128:
                                        fc * QTOK + tok0 + (tc + 1) * 128],
                                identity=identf[:])
                        pe_ot.inc(tr)
                    block.tensor(_tC)

                    def _vE(v, k=k, pe_ot_now=pe_ot.val):
                        v.wait_ge(pe_ot.h, pe_ot_now)
                        sem = s_st[k % 2]
                        v.wait_ge(sem.h, sem.val)
                        ev = v.tensor_copy(out=out_tm[:, (k % 2) * 1024:(k % 2 + 1) * 1024],
                                           in_=ps_ot[:])
                        dve_ot.inc(ev)
                    block.vector(_vE)

                    def _gB(gp, k=k, atok=atok, tc=tc, dve_ot_now=dve_ot.val):
                        gp.wait_ge(dve_ot.h, dve_ot_now)
                        sem = s_st[k % 2]
                        gp.dma_start(out=out_d[seq, atok + tc * 128: atok + (tc + 1) * 128, :],
                                     in_=out_tm[:, (k % 2) * 1024:(k % 2 + 1) * 1024]
                                     ).then_inc(sem.h, 16)
                        sem.val += 16
                    block.gpsimd(_gB)

            barrier()
            L_.close()

    def _gfin(gp):
        for c in [s_const, s_x, s_scr, s_oh] + s_wA + s_wb + s_wc + s_st:
            if c.val:
                gp.wait_ge(c.h, c.val)
    block.gpsimd(_gfin)

    stk.close()
    return nc


_BF = ml_dtypes.bfloat16


def _consts():
    eye = np.eye(128, dtype=np.float32)
    oh32 = np.zeros((32, 32 * 128), dtype=np.float32)
    for e in range(32):
        oh32[e, e * 128:(e + 1) * 128] = 1.0
    oh16 = np.zeros((16, 16 * 128), dtype=np.float32)
    for l in range(16):
        oh16[l, l * 128:(l + 1) * 128] = 0.1
    return {
        "identb": eye.astype(_BF),
        "identf": eye,
        "maskT": np.triu(np.ones((128, 128), dtype=np.float32)),
        "onesc": np.ones((128, 64), dtype=np.float32).astype(_BF),
        "onesr": np.ones((1, 128), dtype=np.float32).astype(_BF),
        "onesrf": np.ones((1, 128), dtype=np.float32),
        "oh32": oh32.astype(_BF),
        "oh16": oh16.astype(_BF),
    }


def kernel(**inputs):
    nc = build_program()
    consts = _consts()
    base = {
        "Wq": inputs["L0_attn_Wq"], "bq": inputs["L0_attn_bq"],
        "Wk": inputs["L0_attn_Wk"], "bk": inputs["L0_attn_bk"],
        "Wv": inputs["L0_attn_Wv"], "bv": inputs["L0_attn_bv"],
        "Wo": inputs["L0_attn_Wo"], "bo": inputs["L0_attn_bo"],
        "gate_w": inputs["L0_ffn_gate_weight"], "gate_b": inputs["L0_ffn_gate_bias"],
        "w_gate": inputs["L0_ffn_w_gate"], "w_up": inputs["L0_ffn_w_up"],
        "w_down": inputs["L0_ffn_w_down"],
        "sw_gate": inputs["L0_ffn_shared_w_gate"], "sw_up": inputs["L0_ffn_shared_w_up"],
        "sw_down": inputs["L0_ffn_shared_w_down"],
        "aw_gate": inputs["L0_ffn_anchor_w_gate"], "aw_up": inputs["L0_ffn_anchor_w_up"],
        "aw_down": inputs["L0_ffn_anchor_w_down"],
    }
    base = {k: np.ascontiguousarray(np.asarray(v, dtype=np.float32)) for k, v in base.items()}
    base.update(consts)
    x = np.asarray(inputs["L0_attn_x"], dtype=np.float32)
    in_maps = []
    for c in range(8):
        m = dict(base)
        m["x"] = np.ascontiguousarray(x[2 * c:2 * c + 2])
        in_maps.append(m)
    res = run_bass_kernel_spmd(nc, in_maps, core_ids=list(range(8)))
    if "dbgb" in res.results[0]:
        np.save("/root/problem/_dev_dbgb.npy", np.asarray(res.results[0]["dbgb"], np.float32))
        np.save("/root/problem/_dev_dbgf.npy", np.asarray(res.results[0]["dbgf"]))
        np.save("/root/problem/_dev_dbgb2.npy", np.asarray(res.results[0]["dbgb2"], np.float32))
    out = np.concatenate([np.asarray(r["out"]) for r in res.results], axis=0)
    return out.astype(np.float32)



# version 13 (final)
# speedup vs baseline: 1.1980x; 1.1980x over previous
import numpy as np
import ml_dtypes
from contextlib import ExitStack
import concourse.bass as bass
import concourse.mybir as mybir
from concourse.bass_utils import run_bass_kernel_spmd

F32 = mybir.dt.float32
BF16 = mybir.dt.bfloat16
AF = mybir.ActivationFunctionType
ALU = mybir.AluOpType
AX = mybir.AxisListType
BFNP = ml_dtypes.bfloat16

D = 1024
NH = 16
TSEQ = 4096
NSEQ = 2
QTOK = 1024
NTILE = 4
TILE = 256
NQ = NSEQ * 4
NEXP = 32
MOE_I = 512
ANC_I = 256
NANC = 16

import os
EMIT_MOE = os.environ.get("KMOE", "1") == "1"
N_Q_EMIT = int(os.environ.get("KNQ", "8"))
N_PAIR_EMIT = int(os.environ.get("KNP", "25"))
KSTAGE = int(os.environ.get("KSTAGE", "9"))
KDBG = os.environ.get("KDBG", "0") == "1"
KSCAN = int(os.environ.get("KSCAN", "9"))


class Ctr:
    def __init__(self, stk, nc, name):
        self.h = stk.enter_context(nc.semaphore(name))
        self.val = 0

    def inc(self, inst, n=1):
        inst.then_inc(self.h, n)
        self.val += n
        return self.val


def build_program():
    nc = bass.Bass("TRN2", target_bir_lowering=False, detect_race_conditions=False)
    stk = ExitStack()

    x_d = nc.dram_tensor("x", [NSEQ, TSEQ, D], F32, kind="ExternalInput").ap()
    out_d = nc.dram_tensor("out", [NSEQ, TSEQ, D], F32, kind="ExternalOutput").ap()
    if KDBG:
        dbgb_d = nc.dram_tensor("dbgb", [128, 8192], BF16, kind="ExternalOutput").ap()
        dbgf_d = nc.dram_tensor("dbgf", [128, 1024], F32, kind="ExternalOutput").ap()
        dbgb2_d = nc.dram_tensor("dbgb2", [128, 2048], BF16, kind="ExternalOutput").ap()

    wq_d = nc.dram_tensor("Wq", [D, D], F32, kind="ExternalInput").ap()
    wk_d = nc.dram_tensor("Wk", [D, D], F32, kind="ExternalInput").ap()
    wv_d = nc.dram_tensor("Wv", [D, D], F32, kind="ExternalInput").ap()
    wo_d = nc.dram_tensor("Wo", [D, D], F32, kind="ExternalInput").ap()
    bq_d = nc.dram_tensor("bq", [D], F32, kind="ExternalInput").ap()
    bk_d = nc.dram_tensor("bk", [D], F32, kind="ExternalInput").ap()
    bv_d = nc.dram_tensor("bv", [D], F32, kind="ExternalInput").ap()
    bo_d = nc.dram_tensor("bo", [D], F32, kind="ExternalInput").ap()
    gw_d = nc.dram_tensor("gate_w", [NEXP, D], F32, kind="ExternalInput").ap()
    gb_d = nc.dram_tensor("gate_b", [NEXP], F32, kind="ExternalInput").ap()
    wg_d = nc.dram_tensor("w_gate", [NEXP, D, MOE_I], F32, kind="ExternalInput").ap()
    wu_d = nc.dram_tensor("w_up", [NEXP, D, MOE_I], F32, kind="ExternalInput").ap()
    wd_d = nc.dram_tensor("w_down", [NEXP, MOE_I, D], F32, kind="ExternalInput").ap()
    wsg_d = nc.dram_tensor("sw_gate", [D, MOE_I], F32, kind="ExternalInput").ap()
    wsu_d = nc.dram_tensor("sw_up", [D, MOE_I], F32, kind="ExternalInput").ap()
    wsd_d = nc.dram_tensor("sw_down", [MOE_I, D], F32, kind="ExternalInput").ap()
    wag_d = nc.dram_tensor("aw_gate", [NANC, D, ANC_I], F32, kind="ExternalInput").ap()
    wau_d = nc.dram_tensor("aw_up", [NANC, D, ANC_I], F32, kind="ExternalInput").ap()
    wad_d = nc.dram_tensor("aw_down", [NANC, ANC_I, D], F32, kind="ExternalInput").ap()

    identb_d = nc.dram_tensor("identb", [128, 128], BF16, kind="ExternalInput").ap()
    identf_d = nc.dram_tensor("identf", [128, 128], F32, kind="ExternalInput").ap()
    maskT_d = nc.dram_tensor("maskT", [128, 128], F32, kind="ExternalInput").ap()
    onesc_d = nc.dram_tensor("onesc", [128, 64], BF16, kind="ExternalInput").ap()
    onesr_d = nc.dram_tensor("onesr", [1, 128], BF16, kind="ExternalInput").ap()
    onesrf_d = nc.dram_tensor("onesrf", [1, 128], F32, kind="ExternalInput").ap()
    oh32_d = nc.dram_tensor("oh32", [32, 32 * 128], BF16, kind="ExternalInput").ap()
    oh16_d = nc.dram_tensor("oh16", [16, 16 * 128], BF16, kind="ExternalInput").ap()

    scr_attn = nc.dram_tensor("scr_attn", [4, D, D], BF16, kind="Internal").ap()
    scr_wg = nc.dram_tensor("scr_wg", [NEXP, 128, 8 * MOE_I], BF16, kind="Internal").ap()
    scr_wu = nc.dram_tensor("scr_wu", [NEXP, 128, 8 * MOE_I], BF16, kind="Internal").ap()
    scr_wd = nc.dram_tensor("scr_wd", [NEXP, 128, 4 * D], BF16, kind="Internal").ap()
    scr_wsg = nc.dram_tensor("scr_wsg", [128, 8 * MOE_I], BF16, kind="Internal").ap()
    scr_wsu = nc.dram_tensor("scr_wsu", [128, 8 * MOE_I], BF16, kind="Internal").ap()
    scr_wsd = nc.dram_tensor("scr_wsd", [128, 4 * D], BF16, kind="Internal").ap()
    scr_wag = nc.dram_tensor("scr_wag", [NANC, 128, 8 * ANC_I], BF16, kind="Internal").ap()
    scr_wau = nc.dram_tensor("scr_wau", [NANC, 128, 8 * ANC_I], BF16, kind="Internal").ap()
    scr_wad = nc.dram_tensor("scr_wad", [NANC, 128, 2 * D], BF16, kind="Internal").ap()

    sb = lambda n, s, d: stk.enter_context(nc.sbuf_tensor(n, s, d))
    y1q = sb("y1q", [128, 8 * QTOK], F32)
    y1qbf = sb("y1qbf", [128, 8 * QTOK], BF16)
    kv_f = sb("kv_f", [128, 512], F32)
    kv_bfe = sb("kv_bfe", [128, 512], BF16)
    kv_bfo = sb("kv_bfo", [128, 512], BF16)
    ksum = sb("ksum", [128, 8], F32)
    ksum_d = sb("ksum_d", [128, 8], F32)
    ksbce = sb("ksbce", [128, 512], BF16)
    ksbco = sb("ksbco", [128, 512], BF16)
    identb = sb("identb_s", [128, 128], BF16)
    identf = sb("identf_s", [128, 128], F32)
    maskT = sb("maskT_s", [128, 128], F32)
    onesc = sb("onesc_s", [128, 64], BF16)
    onesr = sb("onesr_s", [1, 128], BF16)
    onesrf = sb("onesrf_s", [1, 128], F32)
    bias_c = sb("bias_c", [128, 32], F32)
    bv_rbf = sb("bv_rbf", [1, D], BF16)
    gwT = sb("gwT", [128, 8 * 32], F32)
    gb_col = sb("gb_col", [32, 1], F32)
    eps6 = sb("eps6", [128, 1], F32)
    eps5 = sb("eps5", [128, 1], F32)
    junk_a = sb("junk_a", [1, 8], F32)
    junk_v = sb("junk_v", [1, 8], F32)
    cw_em = sb("cw_em", [32, QTOK], BF16)
    acw_em = sb("acw_em", [16, QTOK], BF16)

    C = lambda n: Ctr(stk, nc, n)
    bar = C("bar")
    s_const = C("s_const")
    s_wA = [C("s_wA0"), C("s_wA1")]
    s_x = C("s_x")
    s_wb = [C("s_wb0"), C("s_wb1"), C("s_wb2"), C("s_wb3")]
    s_wc = [C("s_wc0"), C("s_wc1"), C("s_wc2")]
    s_scr = C("s_scr")
    s_st = [C("s_st0"), C("s_st1")]
    s_oh = C("s_oh")
    pe_xt = C("pe_xt"); dve_xt = C("dve_xt"); dve_xbf = C("dve_xbf")
    pe_phi = C("pe_phi"); act_phi = C("act_phi"); dve_phi = C("dve_phi")
    pe_vt = C("pe_vt"); act_vt = C("act_vt")
    pe_kt = C("pe_kt"); pe_st = C("pe_st"); pe_num = C("pe_num")
    pe_den = C("pe_den"); pe_kv = C("pe_kv")
    dve_smt = C("dve_smt"); dve_kt = C("dve_kt"); dve_attn = C("dve_attn")
    dve_state = C("dve_state"); act_den = C("act_den")
    pe_o = C("pe_o"); dve_r1 = C("dve_r1")
    pe_ln = C("pe_ln"); act_ln = C("act_ln"); dve_ln = C("dve_ln")
    pe_rt = C("pe_rt"); act_rt = C("act_rt"); dve_rt = C("dve_rt")
    pe_cwb = C("pe_cwb"); pe_gu = C("pe_gu"); pe_down = C("pe_down"); pe_g = C("pe_g")
    act_silu = C("act_silu"); dve_h = C("dve_h"); dve_evac = C("dve_evac")
    dve_cwb = C("dve_cwb")
    pe_ot = C("pe_ot"); dve_ot = C("dve_ot")
    pe_pro = C("pe_pro"); dve_pro = C("dve_pro")

    bar_n = [0]
    block = stk.enter_context(nc.Block())

    def barrier():
        bar_n[0] += 1
        tgt = 5 * bar_n[0]
        def _s(sync):
            sync.nop().then_inc(bar.h, 1)
            sync.wait_ge(bar.h, tgt)
        block.sync(_s)
        def _g(gp):
            gp.nop().then_inc(bar.h, 1)
            gp.wait_ge(bar.h, tgt)
        block.gpsimd(_g)
        def _t(t):
            t.nop().then_inc(bar.h, 1)
            t.wait_ge(bar.h, tgt)
        block.tensor(_t)
        def _a(a):
            a.copy(out=junk_a[0:1, 0:1], in_=junk_a[0:1, 1:2]).then_inc(bar.h, 1)
            a.wait_ge(bar.h, tgt)
        block.scalar(_a)
        def _v(v):
            v.tensor_copy(out=junk_v[0:1, 2:3], in_=junk_v[0:1, 3:4]).then_inc(bar.h, 1)
            v.wait_ge(bar.h, tgt)
        block.vector(_v)

    subs = [(0, e) for e in range(NEXP)] + [(1, 0)] + [(2, l) for l in range(NANC)]
    pair_of = []
    i = 0
    while i < len(subs):
        if subs[i][0] == 1:
            pair_of.append((i, 1)); i += 1
        else:
            pair_of.append((i, 2)); i += 2
    NSUBS = len(subs)

    def sub_srcs(kind, idx):
        if kind == 0:
            return (wg_d[idx], wu_d[idx], wd_d[idx],
                    scr_wg[idx], scr_wu[idx], scr_wd[idx], MOE_I)
        if kind == 1:
            return (wsg_d, wsu_d, wsd_d, scr_wsg, scr_wsu, scr_wsd, MOE_I)
        return (wag_d[idx], wau_d[idx], wad_d[idx],
                scr_wag[idx], scr_wau[idx], scr_wad[idx], ANC_I)

    cast_grp = {}
    s_wc[0].val += 64
    cast_plan = []
    for s_i, (kind, idx) in enumerate(subs):
        sg, su, sdn, cg, cu, cd, n_i = sub_srcs(kind, idx)
        grp = 0 if s_i < 4 else (1 if s_i < 20 else 2)
        sem = s_wc[grp]
        for src, dst, inner in [(sg, cg, n_i), (su, cu, n_i), (sdn, cd, 1024)]:
            sem.val += 16
            cast_plan.append((src, dst, inner, sem.h))
        cast_grp[s_i] = (sem, sem.val)
    with nc.sbuf_tensor("pro_stage", [128, 8192], BF16) as pro_stage, \
         nc.sbuf_tensor("pro_stage2", [128, 8192], BF16) as pro_stage2, \
         nc.sbuf_tensor("pro_bias", [8, 128], F32) as pro_bias, \
         nc.sbuf_tensor("pro_gw", [32, D], F32) as pro_gw, \
         nc.sbuf_tensor("pro_bv", [1, D], F32) as bv_row, \
         nc.psum_tensor("ps_pro", [128, 256], F32) as ps_pro:

        def _s(sync):
            for src, dst in [(identb_d, identb), (identf_d, identf), (maskT_d, maskT),
                             (onesc_d, onesc), (onesr_d, onesr), (onesrf_d, onesrf)]:
                sync.dma_start(out=dst[:], in_=src).then_inc(s_const.h, 16)
                s_const.val += 16
            for i, bsrc in enumerate([bq_d, bk_d, bo_d]):
                sync.dma_start(out=pro_bias[:], in_=bsrc.rearrange("(c p) -> c p", p=128)
                               ).then_inc(s_const.h, 16)
                s_const.val += 16
                sync.wait_ge(pe_pro.h, i + 1)
            sync.dma_start(out=bv_row[:], in_=bv_d.rearrange("(one d) -> one d", one=1)
                           ).then_inc(s_const.h, 16)
            s_const.val += 16
            sync.dma_start(out=gb_col[:], in_=gb_d.rearrange("(e one) -> e one", one=1)
                           ).then_inc(s_const.h, 16)
            s_const.val += 16
            sync.dma_start(out=pro_gw[:], in_=gw_d).then_inc(s_const.h, 16)
            s_const.val += 16
            for w in range(4):
                sync.wait_ge(s_wc[0].h, (w + 1) * 16)
                sync.dma_start(out=scr_attn[w].rearrange("(c p) j -> p c j", p=128),
                               in_=(pro_stage if w % 2 == 0 else pro_stage2)[:]
                                   .rearrange("p (c j) -> p c j", j=D)
                               ).then_inc(s_scr.h, 16)
                s_scr.val += 16
        block.sync(_s)

        def _g(gp):
            for w, wsrc in enumerate([wq_d, wk_d, wv_d, wo_d]):
                if w >= 2:
                    gp.wait_ge(s_scr.h, (w - 1) * 16)
                gp.dma_start(out=(pro_stage if w % 2 == 0 else pro_stage2)[:]
                                 .rearrange("p (c j) -> p c j", j=D),
                             in_=wsrc.rearrange("(c p) j -> p c j", p=128)
                             ).then_inc(s_wc[0].h, 16)
            for src, dst, inner, semh in cast_plan:
                gp.dma_start(out=dst.rearrange("p (c i) -> p c i", i=inner),
                             in_=src.rearrange("(c p) i -> p c i", p=128)
                             ).then_inc(semh, 16)
        block.gpsimd(_g)

        def _t(t):
            for i in range(3):
                t.wait_ge(s_const.h, (6 + i + 1) * 16)
                if i > 0:
                    t.wait_ge(dve_pro.h, i)
                tr = t.transpose(out=ps_pro[:, 0:8], in_=pro_bias[0:8, :],
                                 identity=identf[0:8, 0:8])
                pe_pro.inc(tr)
            t.wait_ge(s_const.h, s_const.val)
            for kc in range(8):
                t.wait_ge(dve_pro.h, 3 + kc)
                tr = t.transpose(out=ps_pro[:, 32:64], in_=pro_gw[:, kc * 128:(kc + 1) * 128],
                                 identity=identf[0:32, 0:32])
                pe_pro.inc(tr)
        block.tensor(_t)

        def _a0(a):
            a.memzero(junk_a[:])
        block.scalar(_a0)

        def _v(v):
            v.memset(junk_v[:], 0.0)
            v.memset(eps6[:], 1e-6)
            v.memset(eps5[:], 1e-5)
            for i in range(3):
                v.wait_ge(pe_pro.h, i + 1)
                cp = v.tensor_copy(out=bias_c[:, i * 8:(i + 1) * 8], in_=ps_pro[:, 0:8])
                dve_pro.inc(cp)
            for kc in range(8):
                v.wait_ge(pe_pro.h, 3 + kc + 1)
                cp = v.tensor_copy(out=gwT[:, kc * 32:(kc + 1) * 32], in_=ps_pro[:, 32:64])
                dve_pro.inc(cp)
            v.wait_ge(s_const.h, s_const.val)
            v.tensor_copy(out=bv_rbf[:], in_=bv_row[:])
        block.vector(_v)

        barrier()

    hb = lambda h: (h % 2) * 64
    hcf = lambda h: h // 2
    bq_col = lambda oc: bias_c[:, oc:oc + 1]
    bk_col = lambda oc: bias_c[:, 8 + oc:8 + oc + 1]
    bo_col = lambda oc: bias_c[:, 16 + oc:16 + oc + 1]

    down_after = {}
    scr_after = {}
    wval = {}
    for q in range(min(NQ, N_Q_EMIT)):
        seq, qq = q // 4, q % 4
        qbase = qq * QTOK
        first_q = (q == 0)

        A_ = ExitStack()
        _asb = lambda n, sh, dt: A_.enter_context(nc.sbuf_tensor(n, sh, dt))
        wqT = _asb(f"wqT{q}", [128, 8 * D], BF16)
        wkT = _asb(f"wkT{q}", [128, 8 * D], BF16)
        wvT = _asb(f"wvT{q}", [128, 8 * D], BF16)
        woT = _asb(f"woT{q}", [128, 8 * D], BF16)
        x_tm = _asb(f"x_tm{q}", [128, 2048], F32)
        x_fm = _asb(f"x_fm{q}", [128, 2048], F32)
        x_bf = _asb(f"x_bf{q}", [128, 2048], BF16)
        q_bf = _asb(f"q_bf{q}", [128, 2048], BF16)
        k_bfe = _asb(f"k_bfe{q}", [128, 2048], BF16)
        k_bfo = _asb(f"k_bfo{q}", [128, 2048], BF16)
        vt_bf = _asb(f"vt_bf{q}", [128, 2048], BF16)
        kt_bf = _asb(f"kt_bf{q}", [128, 2048], BF16)
        smt_sb = _asb(f"smt{q}", [128, 2048], BF16)
        rbc_sb = _asb(f"rbc{q}", [128, 1024], F32)
        phi_a = _asb(f"phia{q}", [128, 2048], F32)
        phi_b = _asb(f"phib{q}", [128, 2048], F32)
        sq_bf = _asb(f"sqbf{q}", [128, 2048], BF16)
        attn_bf = _asb(f"attnbf{q}", [128, 2048], BF16)
        rows = _asb(f"rows{q}", [1, 1536], F32)
        if True:
            wa = s_wA[q % 2]
            def _sW(sync, wa=wa, first_q=first_q, wqT=wqT, wkT=wkT, wvT=wvT, woT=woT):
                for wi, wT in enumerate([wqT, wkT, wvT, woT]):
                    if first_q:
                        sync.wait_ge(s_scr.h, (wi + 1) * 16)
                    for kc in range(8):
                        sync.dma_start(out=wT[:, kc * D:(kc + 1) * D],
                                       in_=scr_attn[wi][:, kc * 128:(kc + 1) * 128],
                                       transpose=True).then_inc(wa.h, 16)
                        wa.val += 16
            block.sync(_sW)
            wA_done = wa.val
            w_done = [wa.val - 24 * 16, wa.val - 16 * 16, wa.val - 8 * 16, wa.val]

            for tile in range(NTILE):
                gt = q * NTILE + tile
                tok0 = tile * TILE
                atok = qbase + tok0
                y1reg = y1q[:].rearrange("p (c t) -> p c t", t=QTOK)[:, :, tok0:tok0 + TILE]
                y1breg = y1qbf[:].rearrange("p (c t) -> p c t", t=QTOK)[:, :, tok0:tok0 + TILE]

                S13 = ExitStack()
                ps_qk = S13.enter_context(nc.psum_tensor(f"psqk{q}_{tile}", [128, 2048], F32))
                ps_vt = S13.enter_context(nc.psum_tensor(f"psvt{q}_{tile}", [128, 2048], F32))
                if True:

                    def _g1(gp, atok=atok, gt=gt):
                        if gt > 0:
                            gp.wait_ge(pe_xt.h, 16 * gt)
                        for tc in range(2):
                            gp.dma_start(out=x_tm[:, tc * 1024:(tc + 1) * 1024],
                                         in_=x_d[seq, atok + tc * 128: atok + (tc + 1) * 128, :]
                                         ).then_inc(s_x.h, 16)
                            s_x.val += 16
                    block.gpsimd(_g1)

                    dve_xt_base = dve_xt.val
                    dve_r1_now = dve_r1.val
                    dve_ln_now = dve_ln.val
                    def _t1(t, gt=gt, dve_xt_base=dve_xt_base,
                            dve_r1_now=dve_r1_now, dve_ln_now=dve_ln_now):
                        t.wait_ge(s_x.h, s_x.val)
                        if gt > 0:
                            t.wait_ge(dve_r1.h, dve_r1_now)
                            t.wait_ge(dve_ln.h, dve_ln_now)
                        n = 0
                        for tc in range(2):
                            for fc in range(8):
                                g4 = n // 4
                                if g4 >= 2:
                                    t.wait_ge(dve_xt.h, dve_xt_base + g4 - 1)
                                tr = t.transpose(
                                    out=ps_qk[:, (n % 8) * 128:(n % 8) * 128 + 128],
                                    in_=x_tm[:, tc * 1024 + fc * 128: tc * 1024 + (fc + 1) * 128],
                                    identity=identf[:])
                                if n % 4 == 3:
                                    pe_xt.inc(tr, 4)
                                n += 1
                    block.tensor(_t1)

                    def _v1(v, gt=gt):
                        for g4 in range(4):
                            v.wait_ge(pe_xt.h, 16 * gt + (g4 + 1) * 4)
                            tc = g4 // 2
                            fc0 = (g4 % 2) * 4
                            cp = v.tensor_copy(
                                out=x_fm[:].rearrange("p (c t) -> p c t", t=256)
                                    [:, fc0:fc0 + 4, tc * 128:(tc + 1) * 128],
                                in_=ps_qk[:, (g4 % 2) * 512:(g4 % 2) * 512 + 512]
                                    .rearrange("p (c t) -> p c t", t=128))
                            dve_xt.inc(cp)
                        cb = v.tensor_copy(out=x_bf[:], in_=x_fm[:])
                        dve_xbf.inc(cb)
                    block.vector(_v1)

                    if KSTAGE < 2:
                        S13.close()
                        continue
                    act_phi_base = act_phi.val
                    dve_phi_base = dve_phi.val
                    def _t2(t, gt=gt, act_phi_base=act_phi_base, dve_phi_base=dve_phi_base,
                            wqT=wqT, wkT=wkT):
                        t.wait_ge(wa.h, w_done[0])
                        t.wait_ge(dve_xbf.h, gt + 1)
                        for pi, wT in enumerate([wqT, wkT]):
                            if pi == 1:
                                t.wait_ge(wa.h, w_done[1])
                            for oc in range(8):
                                if pi == 1:
                                    t.wait_ge(act_phi.h, act_phi_base + (oc // 2) * 4 + 2)
                                    t.wait_ge(dve_phi.h, dve_phi_base + (oc // 2) * 4 + 2)
                                for kc in range(8):
                                    mm = t.matmul(out=ps_qk[:, oc * 256:(oc + 1) * 256],
                                                  lhsT=wT[:, kc * D + oc * 128: kc * D + (oc + 1) * 128],
                                                  rhs=x_bf[:, kc * 256:(kc + 1) * 256],
                                                  start=(kc == 0), stop=(kc == 7))
                                pe_phi.inc(mm)
                    block.tensor(_t2)

                    pe_phi_base = pe_phi.val - 16
                    def _a2(a, base=pe_phi_base, act_phi_base=act_phi_base,
                            dve_phi_base=dve_phi_base):
                        for pi in range(2):
                            for pr in range(4):
                                for half in range(2):
                                    oc = pr * 2 + half
                                    bias = bq_col(oc) if pi == 0 else bk_col(oc)
                                    a.wait_ge(pe_phi.h, base + pi * 8 + (oc | 1) + 1)
                                    r = a.activation(out=phi_a[:, oc * 256:(oc + 1) * 256],
                                                     in_=ps_qk[:, oc * 256:(oc + 1) * 256],
                                                     func=AF.Relu, bias=bias)
                                    act_phi.inc(r)
                                for half in range(2):
                                    oc = pr * 2 + half
                                    a.wait_ge(dve_phi.h,
                                              dve_phi_base + pi * 16 + pr * 4 + half + 1)
                                    e = a.activation(out=phi_b[:, oc * 256:(oc + 1) * 256],
                                                     in_=phi_b[:, oc * 256:(oc + 1) * 256],
                                                     func=AF.Exp)
                                    act_phi.inc(e)
                    block.scalar(_a2)

                    def _v2(v, base=pe_phi_base, act_phi_base=act_phi_base,
                            q_bf=q_bf, k_bfe=k_bfe, k_bfo=k_bfo, tile=tile):
                        if tile == 0:
                            v.memset(k_bfe[64:128, :], 0.0)
                            v.memset(k_bfo[0:64, :], 0.0)
                        for pi, dst in [(0, q_bf), (1, None)]:
                            for pr in range(4):
                                for half in range(2):
                                    oc = pr * 2 + half
                                    bias = bq_col(oc) if pi == 0 else bk_col(oc)
                                    v.wait_ge(pe_phi.h, base + pi * 8 + (oc | 1) + 1)
                                    v.wait_ge(act_phi.h, act_phi_base + pi * 16 + pr * 4 + 2)
                                    m = v.tensor_scalar(out=phi_b[:, oc * 256:(oc + 1) * 256],
                                                        in0=ps_qk[:, oc * 256:(oc + 1) * 256],
                                                        scalar1=bias, scalar2=0.0,
                                                        op0=ALU.add, op1=ALU.min)
                                    dve_phi.inc(m)
                                for half in range(2):
                                    oc = pr * 2 + half
                                    v.wait_ge(act_phi.h,
                                              act_phi_base + pi * 16 + pr * 4 + 3 + half)
                                    if pi == 0:
                                        sdd = v.tensor_add(out=dst[:, oc * 256:(oc + 1) * 256],
                                                           in0=phi_a[:, oc * 256:(oc + 1) * 256],
                                                           in1=phi_b[:, oc * 256:(oc + 1) * 256])
                                    else:
                                        v.tensor_add(out=k_bfe[0:64, oc * 256:(oc + 1) * 256],
                                                     in0=phi_a[0:64, oc * 256:(oc + 1) * 256],
                                                     in1=phi_b[0:64, oc * 256:(oc + 1) * 256])
                                        sdd = v.tensor_add(out=k_bfo[64:128, oc * 256:(oc + 1) * 256],
                                                           in0=phi_a[64:128, oc * 256:(oc + 1) * 256],
                                                           in1=phi_b[64:128, oc * 256:(oc + 1) * 256])
                                    dve_phi.inc(sdd)
                    block.vector(_v2)

                    def _t3(t, wvT=wvT):
                        t.wait_ge(wa.h, w_done[2])
                        for tc in range(2):
                            for half in range(2):
                                sl = tc * 2 + half
                                for kc in range(8):
                                    t.matmul(out=ps_vt[:, sl * 512:(sl + 1) * 512],
                                             lhsT=x_bf[:, kc * 256 + tc * 128: kc * 256 + (tc + 1) * 128],
                                             rhs=wvT[:, kc * D + half * 512: kc * D + (half + 1) * 512],
                                             start=(kc == 0), stop=False)
                                mm = t.matmul(out=ps_vt[:, sl * 512:(sl + 1) * 512],
                                              lhsT=onesr[:],
                                              rhs=bv_rbf[0:1, half * 512:(half + 1) * 512],
                                              start=False, stop=True)
                                pe_vt.inc(mm)
                    block.tensor(_t3)

                    pe_vt_base = pe_vt.val - 4
                    def _a3(a, base=pe_vt_base, vt_bf=vt_bf):
                        for sl in range(4):
                            a.wait_ge(pe_vt.h, base + sl + 1)
                            tc, half = sl // 2, sl % 2
                            c = a.activation(out=vt_bf[:, tc * 1024 + half * 512:
                                                       tc * 1024 + (half + 1) * 512],
                                             in_=ps_vt[:, sl * 512:(sl + 1) * 512],
                                             func=AF.Identity)
                            act_vt.inc(c)
                    block.scalar(_a3)

                S13.close()
                if KSTAGE < 4:
                    continue
                S4_ = ExitStack()
                ps_sT = S4_.enter_context(nc.psum_tensor(f"psst{q}_{tile}", [128, 2048], F32))
                ps_n = S4_.enter_context(nc.psum_tensor(f"psn{q}_{tile}", [128, 1024], F32))
                ps_d = ps_sT
                ps_kt = S4_.enter_context(nc.psum_tensor(f"pskt{q}_{tile}", [128, 1024], BF16))
                ps_kt2 = S4_.enter_context(nc.psum_tensor(f"pskt2{q}_{tile}", [128, 1024], BF16))
                ps_kv = ps_sT[:, 1024:1536]
                if True:

                    for tc in range(2):
                        cg = q * 8 + tile * 2 + tc
                        smt_fut = dve_smt.val + 1
                        kt_fut = dve_kt.val + 2
                        first_scan = (tc == 0 and tile == 0 and qq == 0)
                        state_now = dve_state.val + (1 if first_scan else 0)
                        act_vt_now = act_vt.val
                        dve_phi_now = dve_phi.val
                        act_phi_now = act_phi.val
                        dve_attn_now = dve_attn.val
                        act_den_now = act_den.val

                        def _t4(t, tc=tc, cg=cg, tile=tile, smt_fut=smt_fut, kt_fut=kt_fut,
                                _dump_gate=(KDBG and q == 0 and tile == 0 and tc == 1),
                                state_now=state_now, act_vt_now=act_vt_now,
                                dve_phi_now=dve_phi_now, act_phi_now=act_phi_now,
                                dve_attn_now=dve_attn_now, act_den_now=act_den_now,
                                k_bfe=k_bfe, k_bfo=k_bfo, q_bf=q_bf, vt_bf=vt_bf,
                                kt_bf=kt_bf, smt_sb=smt_sb):
                            if tc == 0:
                                t.wait_ge(act_vt.h, act_vt_now)
                                t.wait_ge(dve_phi.h, dve_phi_now)
                                t.wait_ge(act_phi.h, act_phi_now)
                            if _dump_gate:
                                t.wait_ge(s_st[0].h, 9 * 16)
                            if cg > 0:
                                if KSCAN >= 5:
                                    t.wait_ge(dve_state.h, state_now)
                                if KSCAN >= 2:
                                    t.wait_ge(dve_smt.h, smt_fut - 1)
                                if KSCAN >= 1:
                                    t.wait_ge(dve_kt.h, kt_fut - 2)
                                if KSCAN >= 4:
                                    t.wait_ge(dve_attn.h, dve_attn_now)
                                if KSCAN >= 3:
                                    t.wait_ge(act_den.h, act_den_now)
                            if KSCAN >= 1:
                              for fc in range(8):
                                t.transpose(out=ps_kt[0:64, fc * 128:(fc + 1) * 128].bitcast(BF16)
                                            if False else ps_kt[:, fc * 128:(fc + 1) * 128],
                                            in_=k_bfe[:, fc * 256 + tc * 128: fc * 256 + (tc + 1) * 128],
                                            identity=identb[:])
                                tr = t.transpose(out=ps_kt2[:, fc * 128:(fc + 1) * 128],
                                                 in_=k_bfo[:, fc * 256 + tc * 128: fc * 256 + (tc + 1) * 128],
                                                 identity=identb[:])
                              pe_kt.inc(tr)
                            if KSCAN >= 2:
                              for h in range(NH):
                                kp = k_bfe if h % 2 == 0 else k_bfo
                                mm = t.matmul(out=ps_sT[:, h * 128:(h + 1) * 128],
                                              lhsT=kp[:, hcf(h) * 256 + tc * 128: hcf(h) * 256 + (tc + 1) * 128],
                                              rhs=q_bf[:, hcf(h) * 256 + tc * 128: hcf(h) * 256 + (tc + 1) * 128],
                                              start=True, stop=True)
                              pe_st.inc(mm)
                            if KSCAN >= 5:
                              t.wait_ge(dve_kt.h, kt_fut)
                              t.wait_ge(dve_state.h, state_now)
                              for h in range(NH):
                                mm = t.matmul(out=ps_sT[hb(h):hb(h) + 64, 1024 + hcf(h) * 64: 1024 + (hcf(h) + 1) * 64],
                                              lhsT=kt_bf[:, (h % 2) * 1024 + hcf(h) * 128 + (h % 2) * 64:
                                                         (h % 2) * 1024 + hcf(h) * 128 + (h % 2) * 64 + 64],
                                              rhs=vt_bf[:, tc * 1024 + h * 64: tc * 1024 + (h + 1) * 64],
                                              start=True, stop=True,
                                              tile_position=(0, hb(h)))
                              pe_kv.inc(mm)
                            if KSCAN >= 4:
                              t.wait_ge(dve_smt.h, smt_fut)
                              t.wait_ge(dve_state.h, state_now)
                              for h in range(NH):
                                t.matmul(out=ps_n[hb(h):hb(h) + 64, hcf(h) * 128:(hcf(h) + 1) * 128],
                                         lhsT=vt_bf[:, tc * 1024 + h * 64: tc * 1024 + (h + 1) * 64],
                                         rhs=smt_sb[:, h * 128:(h + 1) * 128],
                                         start=True, stop=False,
                                         tile_position=(0, hb(h)))
                                kvp = kv_bfe if h % 2 == 0 else kv_bfo
                                mm = t.matmul(out=ps_n[hb(h):hb(h) + 64, hcf(h) * 128:(hcf(h) + 1) * 128],
                                              lhsT=kvp[:, hcf(h) * 64:(hcf(h) + 1) * 64],
                                              rhs=q_bf[:, hcf(h) * 256 + tc * 128: hcf(h) * 256 + (tc + 1) * 128],
                                              start=False, stop=True,
                                              tile_position=(0, hb(h)))
                            pe_num.inc(mm)
                            if KSCAN >= 3:
                              if KSCAN < 4:
                                t.wait_ge(dve_smt.h, smt_fut)
                              for h in range(NH):
                                t.matmul(out=ps_d[hb(h):hb(h) + 64, hcf(h) * 128:(hcf(h) + 1) * 128],
                                         lhsT=onesc[:],
                                         rhs=smt_sb[:, h * 128:(h + 1) * 128],
                                         start=True, stop=False,
                                         tile_position=(0, hb(h)))
                                ksp = ksbce if h % 2 == 0 else ksbco
                                mm = t.matmul(out=ps_d[hb(h):hb(h) + 64, hcf(h) * 128:(hcf(h) + 1) * 128],
                                              lhsT=ksp[:, hcf(h) * 64:(hcf(h) + 1) * 64],
                                              rhs=q_bf[:, hcf(h) * 256 + tc * 128: hcf(h) * 256 + (tc + 1) * 128],
                                              start=False, stop=True,
                                              tile_position=(0, hb(h)))
                              pe_den.inc(mm)
                        block.tensor(_t4)

                        def _a4(a, cg=cg, rbc_sb=rbc_sb):
                            if KSCAN < 3:
                                return
                            a.wait_ge(pe_den.h, cg + 1)
                            a.activation(out=rbc_sb[:], in_=ps_d[:, 0:1024], func=AF.Ln, bias=eps6[:])
                            e = a.activation(out=rbc_sb[:], in_=rbc_sb[:], func=AF.Exp, scale=-1.0)
                            act_den.inc(e)
                        block.scalar(_a4)

                        def _v4(v, tc=tc, cg=cg, first_scan=first_scan,
                                k_bfe=k_bfe, k_bfo=k_bfo, kt_bf=kt_bf, smt_sb=smt_sb, attn_bf=attn_bf,
                                rbc_sb=rbc_sb):
                            if first_scan:
                                v.memset(kv_f[:], 0.0)
                                v.memset(kv_bfe[:], 0.0)
                                v.memset(kv_bfo[:], 0.0)
                                v.memset(ksum[:], 0.0)
                                v.memset(ksbce[:], 0.0)
                                ms = v.memset(ksbco[:], 0.0)
                                dve_state.inc(ms)
                            if KSCAN >= 2:
                                v.wait_ge(pe_st.h, cg + 1)
                                sm = v.tensor_tensor(
                                    out=smt_sb[:].rearrange("p (h t) -> p h t", t=128),
                                    in0=ps_sT[:].rearrange("p (h t) -> p h t", t=128),
                                    in1=maskT[:].unsqueeze(1).broadcast_to((128, 16, 128)),
                                    op=ALU.mult)
                                dve_smt.inc(sm)
                            if KSCAN >= 1:
                                v.wait_ge(pe_kt.h, cg + 1)
                                kc_ = v.tensor_copy(out=kt_bf[:, 0:1024], in_=ps_kt[:])
                                dve_kt.inc(kc_)
                                kc_ = v.tensor_copy(out=kt_bf[:, 1024:2048], in_=ps_kt2[:])
                                dve_kt.inc(kc_)
                            if KSCAN >= 5:
                                v.wait_ge(pe_kv.h, cg + 1)
                                v.reduce_sum(out=ksum_d[0:64, :],
                                             in_=k_bfe[0:64, :].rearrange("p (c t) -> p c t", t=256)
                                                 [:, :, tc * 128:(tc + 1) * 128],
                                             axis=AX.X)
                                v.reduce_sum(out=ksum_d[64:128, :],
                                             in_=k_bfo[64:128, :].rearrange("p (c t) -> p c t", t=256)
                                                 [:, :, tc * 128:(tc + 1) * 128],
                                             axis=AX.X)
                                v.tensor_add(out=kv_f[:], in0=ps_kv, in1=kv_f[:])
                                v.tensor_add(out=ksum[:], in0=ksum[:], in1=ksum_d[:])
                                v.tensor_copy(out=kv_bfe[0:64, :], in_=kv_f[0:64, :])
                                v.tensor_copy(out=kv_bfo[64:128, :], in_=kv_f[64:128, :])
                                for c in range(8):
                                    v.tensor_tensor(out=ksbce[0:64, c * 64:(c + 1) * 64],
                                                    in0=onesc[0:64, 0:64],
                                                    in1=ksum[0:64, c:c + 1].broadcast_to((64, 64)),
                                                    op=ALU.mult)
                                    u3 = v.tensor_tensor(out=ksbco[64:128, c * 64:(c + 1) * 64],
                                                         in0=onesc[64:128, 0:64],
                                                         in1=ksum[64:128, c:c + 1].broadcast_to((64, 64)),
                                                         op=ALU.mult)
                                dve_state.inc(u3, 4)
                            if KSCAN >= 4:
                                v.wait_ge(pe_num.h, cg + 1)
                                v.wait_ge(act_den.h, cg + 1)
                                at = v.tensor_tensor(
                                    out=attn_bf[:].rearrange("p (c t) -> p c t", t=256)
                                        [:, :, tc * 128:(tc + 1) * 128],
                                    in0=ps_n[:].rearrange("p (c t) -> p c t", t=128),
                                    in1=rbc_sb[:].rearrange("p (c t) -> p c t", t=128),
                                    op=ALU.mult)
                                dve_attn.inc(at)
                        block.vector(_v4)

                        if KDBG and q == 0 and tile == 0 and tc == 0:
                            _dsv = dve_state.val
                            def _gdbg(gp, _dsv=_dsv):
                                gp.wait_ge(dve_state.h, _dsv)
                                gp.dma_start(out=dbgb_d[:, 0:2048], in_=kt_bf[:]).then_inc(s_st[0].h, 16)
                                gp.dma_start(out=dbgb_d[:, 2048:2560], in_=kv_bfe[:]).then_inc(s_st[0].h, 16)
                                gp.dma_start(out=dbgb_d[:, 2560:3072], in_=kv_bfo[:]).then_inc(s_st[0].h, 16)
                                gp.dma_start(out=dbgb_d[:, 3072:5120], in_=smt_sb[:]).then_inc(s_st[0].h, 16)
                                gp.dma_start(out=dbgb_d[:, 5120:5632], in_=ksbce[:]).then_inc(s_st[0].h, 16)
                                gp.dma_start(out=dbgb_d[:, 5632:6144], in_=ksbco[:]).then_inc(s_st[0].h, 16)
                                gp.dma_start(out=dbgb_d[:, 6144:8192], in_=vt_bf[:]).then_inc(s_st[0].h, 16)
                                gp.dma_start(out=dbgf_d[:, 0:512], in_=kv_f[:]).then_inc(s_st[0].h, 16)
                                gp.dma_start(out=dbgf_d[:, 512:520], in_=ksum[:]).then_inc(s_st[0].h, 16)
                                for _ in range(9):
                                    s_st[0].val += 16
                            block.gpsimd(_gdbg)

                S4_.close()
                if KSTAGE < 5:
                    continue
                S5_ = ExitStack()
                ps_o = S5_.enter_context(nc.psum_tensor(f"pso{q}_{tile}", [128, 2048], F32))
                ps_stat = S5_.enter_context(nc.psum_tensor(f"psstat{q}_{tile}", [1, 512], F32))
                ps_bc = S5_.enter_context(nc.psum_tensor(f"psbc{q}_{tile}", [128, 512], F32))
                if True:

                    pe_o_base = pe_o.val
                    dve_r1_base = dve_r1.val
                    act_ln_base = act_ln.val
                    dve_ln_base = dve_ln.val
                    pe_ln_base = pe_ln.val

                    def _t5(t, pe_o_base=pe_o_base, dve_r1_base=dve_r1_base,
                            act_ln_base=act_ln_base, dve_ln_base=dve_ln_base,
                            woT=woT, attn_bf=attn_bf, q_bf=q_bf, sq_bf=sq_bf, rows=rows):
                        t.wait_ge(dve_smt.h, dve_smt.val)
                        t.wait_ge(dve_kt.h, dve_kt.val)
                        t.wait_ge(dve_attn.h, dve_attn.val)
                        t.wait_ge(dve_state.h, dve_state.val)
                        t.wait_ge(act_den.h, act_den.val)
                        t.wait_ge(wa.h, w_done[3])
                        for oc in range(8):
                            for kc in range(8):
                                mm = t.matmul(out=ps_o[:, oc * 256:(oc + 1) * 256],
                                              lhsT=woT[:, kc * D + oc * 128: kc * D + (oc + 1) * 128],
                                              rhs=attn_bf[:, kc * 256:(kc + 1) * 256],
                                              start=(kc == 0), stop=(kc == 7))
                            pe_o.inc(mm)
                        t.wait_ge(dve_r1.h, dve_r1_base + 9)
                        t.wait_ge(act_ln.h, act_ln_base + 1)
                        for fc in range(8):
                            t.matmul(out=ps_stat[0:1, 0:256], lhsT=onesc[:, 0:1],
                                     rhs=q_bf[:, fc * 256:(fc + 1) * 256],
                                     start=(fc == 0), stop=(fc == 7))
                        for fc in range(8):
                            mm = t.matmul(out=ps_stat[0:1, 256:512], lhsT=onesc[:, 0:1],
                                          rhs=sq_bf[:, fc * 256:(fc + 1) * 256],
                                          start=(fc == 0), stop=(fc == 7))
                        pe_ln.inc(mm)
                        t.wait_ge(act_ln.h, act_ln_base + 3)
                        t.wait_ge(dve_ln.h, dve_ln_base + 4)
                        t.matmul(out=ps_bc[:, 0:256], lhsT=onesrf[:], rhs=rows[0:1, 512:768],
                                 start=True, stop=True)
                        mm = t.matmul(out=ps_bc[:, 256:512], lhsT=onesrf[:], rhs=rows[0:1, 768:1024],
                                      start=True, stop=True)
                        pe_ln.inc(mm)
                    block.tensor(_t5)

                    def _a5(a, dve_r1_base=dve_r1_base, dve_ln_base=dve_ln_base,
                            sq_bf=sq_bf, rows=rows, y1reg=y1reg):
                        a.wait_ge(dve_r1.h, dve_r1_base + 9)
                        s = a.activation(out=sq_bf[:].rearrange("p (c t) -> p c t", t=256),
                                         in_=y1reg, func=AF.Square)
                        act_ln.inc(s)
                        a.wait_ge(dve_ln.h, dve_ln_base + 3)
                        l = a.activation(out=rows[0:1, 256:512], in_=rows[0:1, 0:256],
                                         func=AF.Ln, bias=eps5[0:1, :])
                        act_ln.inc(l)
                        r = a.activation(out=rows[0:1, 512:768], in_=rows[0:1, 256:512],
                                         func=AF.Exp, scale=-0.5)
                        act_ln.inc(r)
                    block.scalar(_a5)

                    def _v5(v, tok0=tok0, pe_o_base=pe_o_base, pe_ln_base=pe_ln_base,
                            act_ln_base=act_ln_base,
                            q_bf=q_bf, rows=rows, x_fm=x_fm, y1reg=y1reg, y1breg=y1breg):
                        for oc in range(8):
                            v.wait_ge(pe_o.h, pe_o_base + (oc | 1) + 1)
                            r = v.scalar_tensor_tensor(
                                out=y1q[:, oc * QTOK + tok0: oc * QTOK + tok0 + TILE],
                                in0=ps_o[:, oc * 256:(oc + 1) * 256],
                                scalar=bo_col(oc), in1=x_fm[:, oc * 256:(oc + 1) * 256],
                                op0=ALU.add, op1=ALU.add)
                            dve_r1.inc(r)
                        rb = v.tensor_copy(out=q_bf[:].rearrange("p (c t) -> p c t", t=256),
                                           in_=y1reg)
                        dve_r1.inc(rb)
                        v.wait_ge(pe_ln.h, pe_ln_base + 1)
                        mu = v.tensor_scalar(out=rows[0:1, 1024:1280], in0=ps_stat[0:1, 0:256],
                                             scalar1=1.0 / D, scalar2=None, op0=ALU.mult)
                        dve_ln.inc(mu)
                        m2 = v.tensor_mul(out=rows[0:1, 1280:1536], in0=rows[0:1, 1024:1280],
                                          in1=rows[0:1, 1024:1280])
                        dve_ln.inc(m2)
                        var = v.scalar_tensor_tensor(out=rows[0:1, 0:256],
                                                     in0=ps_stat[0:1, 256:512],
                                                     scalar=1.0 / D, in1=rows[0:1, 1280:1536],
                                                     op0=ALU.mult, op1=ALU.subtract)
                        dve_ln.inc(var)
                        v.wait_ge(act_ln.h, act_ln_base + 3)
                        p = v.tensor_mul(out=rows[0:1, 768:1024], in0=rows[0:1, 1024:1280],
                                         in1=rows[0:1, 512:768])
                        dve_ln.inc(p)
                        v.wait_ge(pe_ln.h, pe_ln_base + 2)
                        n1 = v.tensor_tensor(out=y1reg, in0=y1reg,
                                             in1=ps_bc[:, 0:256].unsqueeze(1)
                                                 .broadcast_to((128, 8, 256)),
                                             op=ALU.mult)
                        dve_ln.inc(n1)
                        n2 = v.tensor_tensor(out=y1reg, in0=y1reg,
                                             in1=ps_bc[:, 256:512].unsqueeze(1)
                                                 .broadcast_to((128, 8, 256)),
                                             op=ALU.subtract)
                        dve_ln.inc(n2)
                        yb = v.tensor_copy(out=y1breg, in_=y1reg)
                        dve_ln.inc(yb)
                    block.vector(_v5)
                S5_.close()

            barrier()
            A_.close()

        if not EMIT_MOE:
            with nc.sbuf_tensor(f"douttm{q}", [128, 2048], F32) as out_tm_dbg, \
                 nc.psum_tensor(f"dpsot{q}", [128, 1024], F32) as ps_ot_dbg:
                for tile in range(NTILE):
                    tok0 = tile * TILE
                    atok = qbase + tok0
                    for tc in range(2):
                        k = tile * 2 + tc
                        def _tD(t, tc=tc, tok0=tok0, dve_ot_now=dve_ot.val):
                            t.wait_ge(dve_ot.h, dve_ot_now)
                            for fc in range(8):
                                tr = t.transpose(
                                    out=ps_ot_dbg[:, fc * 128:(fc + 1) * 128],
                                    in_=y1q[:, fc * QTOK + tok0 + tc * 128:
                                            fc * QTOK + tok0 + (tc + 1) * 128],
                                    identity=identf[:])
                            pe_ot.inc(tr)
                        block.tensor(_tD)
                        def _vD2(v, k=k, pe_ot_now=pe_ot.val):
                            v.wait_ge(pe_ot.h, pe_ot_now)
                            sem = s_st[k % 2]
                            v.wait_ge(sem.h, sem.val)
                            ev = v.tensor_copy(out=out_tm_dbg[:, (k % 2) * 1024:(k % 2 + 1) * 1024],
                                               in_=ps_ot_dbg[:])
                            dve_ot.inc(ev)
                        block.vector(_vD2)
                        def _gD(gp, k=k, atok=atok, tc=tc, dve_ot_now=dve_ot.val):
                            gp.wait_ge(dve_ot.h, dve_ot_now)
                            sem = s_st[k % 2]
                            gp.dma_start(out=out_d[seq, atok + tc * 128: atok + (tc + 1) * 128, :],
                                         in_=out_tm_dbg[:, (k % 2) * 1024:(k % 2 + 1) * 1024]
                                         ).then_inc(sem.h, 16)
                            sem.val += 16
                        block.gpsimd(_gD)
            barrier()
            continue

        R_ = ExitStack()
        ps_rt = R_.enter_context(nc.psum_tensor(f"psrt{q}", [32, 256], F32))
        ps_rtt = R_.enter_context(nc.psum_tensor(f"psrtt{q}", [128, 1024], F32))
        sc_em = R_.enter_context(nc.sbuf_tensor(f"rt_sc{q}", [128, 256], F32))
        sco_em = R_.enter_context(nc.sbuf_tensor(f"rt_sco{q}", [128, 256], F32))
        rt_tm = R_.enter_context(nc.sbuf_tensor(f"rt_tm{q}", [128, 384], F32))
        rt_wk = R_.enter_context(nc.sbuf_tensor(f"rt_wk{q}", [128, 96], F32))
        rt_cols = R_.enter_context(nc.sbuf_tensor(f"rt_cols{q}", [128, 16], F32))
        if True:
            for tile in range(NTILE):
                tok0 = tile * TILE
                pe_rt_b = pe_rt.val
                act_rt_b = act_rt.val
                dve_rt_b = dve_rt.val

                def _t6(t, tok0=tok0, first=(tile == 0), act_rt_now=act_rt.val):
                    if not first:
                        t.wait_ge(act_rt.h, act_rt_now)
                    for kc in range(8):
                        mm = t.matmul(out=ps_rt[:], lhsT=gwT[:, kc * 32:(kc + 1) * 32],
                                      rhs=y1q[:, kc * QTOK + tok0: kc * QTOK + tok0 + TILE],
                                      start=(kc == 0), stop=(kc == 7))
                    pe_rt.inc(mm)
                block.tensor(_t6)

                def _a6(a, pe_rt_b=pe_rt_b, sco_em=sco_em):
                    a.wait_ge(pe_rt.h, pe_rt_b + 1)
                    s = a.activation(out=sco_em[0:32, :], in_=ps_rt[:], func=AF.Sigmoid)
                    act_rt.inc(s)
                block.scalar(_a6)

                def _v6(v, act_rt_b=act_rt_b, sc_em=sc_em, sco_em=sco_em):
                    v.wait_ge(act_rt.h, act_rt_b + 1)
                    if tile == 0:
                        v.memset(rt_tm[:], 0.0)
                        v.memset(rt_wk[:], 0.0)
                        v.memset(rt_cols[:], 0.0)
                        v.memset(sc_em[32:64, :], 0.0)
                        v.memset(sc_em[64:128, :], 0.0)
                        v.memset(sco_em[32:64, :], 0.0)
                        v.memset(sco_em[64:128, :], 0.0)
                    s = v.tensor_scalar(out=sc_em[0:32, :], in0=sco_em[0:32, :], scalar1=gb_col[:],
                                        scalar2=None, op0=ALU.add)
                    dve_rt.inc(s)
                block.vector(_v6)

                def _t7(t, dve_rt_now=dve_rt.val, sc_em=sc_em, sco_em=sco_em):
                    t.wait_ge(dve_rt.h, dve_rt_now)
                    for tc in range(2):
                        t.transpose(out=ps_rtt[:, tc * 256: tc * 256 + 128],
                                    in_=sc_em[:, tc * 128:(tc + 1) * 128],
                                    identity=identf[:])
                        tr = t.transpose(out=ps_rtt[:, tc * 256 + 128: tc * 256 + 256],
                                         in_=sco_em[:, tc * 128:(tc + 1) * 128],
                                         identity=identf[:])
                    pe_rt.inc(tr)
                block.tensor(_t7)

                def _v7(v, pe_rt_now=pe_rt.val):
                    TCS = (0, 1)
                    F = lambda tc, a, b: rt_tm[:, tc * 192 + a: tc * 192 + b]
                    W = lambda tc, a, b: rt_wk[:, tc * 48 + a: tc * 48 + b]
                    C = lambda tc, i: rt_cols[:, tc * 8 + i: tc * 8 + i + 1]
                    bc = lambda col, n: col.broadcast_to((128, n))
                    v.wait_ge(pe_rt.h, pe_rt_now)
                    for tc in TCS:
                        v.tensor_copy(out=F(tc, 0, 32), in_=ps_rtt[:, tc * 256: tc * 256 + 32])
                    for tc in TCS:
                        v.tensor_copy(out=F(tc, 32, 64),
                                      in_=ps_rtt[:, tc * 256 + 128: tc * 256 + 160])
                    for g in range(4):
                        for tc in TCS:
                            v.reduce_max(out=C(tc, 0), in_=F(tc, 8 * g, 8 * g + 8), axis=AX.X)
                        for tc in TCS:
                            v.tensor_tensor(out=W(tc, 0, 8), in0=F(tc, 8 * g, 8 * g + 8),
                                            in1=bc(C(tc, 0), 8), op=ALU.is_equal)
                        for tc in TCS:
                            v.scalar_tensor_tensor(out=W(tc, 0, 8), in0=W(tc, 0, 8),
                                                   scalar=-1e30, in1=F(tc, 8 * g, 8 * g + 8),
                                                   op0=ALU.mult, op1=ALU.add)
                        for tc in TCS:
                            v.reduce_max(out=C(tc, 1), in_=W(tc, 0, 8), axis=AX.X)
                        for tc in TCS:
                            v.tensor_add(out=F(tc, 176, 177)[:, 0:1] if False else
                                         rt_tm[:, tc * 192 + 176 + g: tc * 192 + 177 + g],
                                         in0=C(tc, 0), in1=C(tc, 1))
                    for tc in TCS:
                        v.reduce_max(out=C(tc, 0), in_=F(tc, 176, 180), axis=AX.X)
                    for tc in TCS:
                        v.tensor_tensor(out=W(tc, 8, 12), in0=F(tc, 176, 180),
                                        in1=bc(C(tc, 0), 4), op=ALU.is_equal)
                    for tc in TCS:
                        v.scalar_tensor_tensor(out=W(tc, 8, 12), in0=W(tc, 8, 12),
                                               scalar=-1e30, in1=F(tc, 176, 180),
                                               op0=ALU.mult, op1=ALU.add)
                    for tc in TCS:
                        v.reduce_max(out=C(tc, 1), in_=W(tc, 8, 12), axis=AX.X)
                    for tc in TCS:
                        v.tensor_tensor(out=W(tc, 8, 12), in0=F(tc, 176, 180),
                                        in1=bc(C(tc, 1), 4), op=ALU.is_ge)
                    for tc in TCS:
                        v.tensor_scalar(out=W(tc, 8, 12), in0=W(tc, 8, 12), scalar1=1.0,
                                        scalar2=1e30, op0=ALU.subtract, op1=ALU.mult)
                    for g in range(4):
                        for tc in TCS:
                            v.tensor_tensor(out=F(tc, 64 + 8 * g, 72 + 8 * g),
                                            in0=F(tc, 8 * g, 8 * g + 8),
                                            in1=bc(rt_wk[:, tc * 48 + 8 + g: tc * 48 + 9 + g], 8),
                                            op=ALU.add)
                    for tc in TCS:
                        v.tensor_copy(out=F(tc, 96, 128), in_=F(tc, 64, 96))
                    for tc in TCS:
                        v.reduce_max(out=C(tc, 2), in_=F(tc, 96, 128), axis=AX.X)
                    for it in range(5):
                        for tc in TCS:
                            src = C(tc, 2) if it == 0 else C(tc, 1)
                            v.tensor_tensor(out=W(tc, 16, 48), in0=F(tc, 96, 128),
                                            in1=bc(src, 32), op=ALU.is_equal)
                        for tc in TCS:
                            v.scalar_tensor_tensor(out=F(tc, 96, 128), in0=W(tc, 16, 48),
                                                   scalar=-1e30, in1=F(tc, 96, 128),
                                                   op0=ALU.mult, op1=ALU.add)
                        if it < 4:
                            for tc in TCS:
                                v.reduce_max(out=C(tc, 1), in_=F(tc, 96, 128), axis=AX.X)
                    for tc in TCS:
                        v.reduce_max(out=C(tc, 3), in_=F(tc, 96, 128), axis=AX.X)
                    for tc in TCS:
                        v.tensor_tensor(out=W(tc, 16, 48), in0=F(tc, 64, 96),
                                        in1=bc(C(tc, 3), 32), op=ALU.is_ge)
                    for tc in TCS:
                        v.tensor_mul(out=W(tc, 16, 48), in0=F(tc, 32, 64), in1=W(tc, 16, 48))
                    for tc in TCS:
                        v.reduce_sum(out=C(tc, 4), in_=W(tc, 16, 48), axis=AX.X)
                    for tc in TCS:
                        v.tensor_scalar(out=C(tc, 4), in0=C(tc, 4), scalar1=1e-20,
                                        scalar2=None, op0=ALU.add)
                    for tc in TCS:
                        v.reciprocal(out=C(tc, 4), in_=C(tc, 4))
                    for tc in TCS:
                        v.tensor_scalar(out=C(tc, 4), in0=C(tc, 4), scalar1=0.2,
                                        scalar2=None, op0=ALU.mult)
                    for tc in TCS:
                        v.tensor_tensor(out=F(tc, 128, 160), in0=W(tc, 16, 48),
                                        in1=bc(C(tc, 4), 32), op=ALU.mult)
                    for tc in TCS:
                        v.tensor_tensor(out=W(tc, 16, 48), in0=F(tc, 64, 96),
                                        in1=bc(C(tc, 2), 32), op=ALU.is_ge)
                    last = None
                    for tc in TCS:
                        w2 = W(tc, 16, 48).rearrange("p (l two) -> p l two", two=2)
                        last = v.tensor_add(out=F(tc, 160, 176), in0=w2[:, :, 0:1].squeeze(2),
                                            in1=w2[:, :, 1:2].squeeze(2))
                    dve_rt.inc(last)
                block.vector(_v7)

                def _t8(t, dve_rt_now=dve_rt.val):
                    t.wait_ge(dve_rt.h, dve_rt_now)
                    for tc in range(2):
                        t.transpose(out=ps_rtt[0:32, 512 + tc * 256: 512 + tc * 256 + 128],
                                    in_=rt_tm[:, tc * 192 + 128: tc * 192 + 160],
                                    identity=identf[:])
                        tr = t.transpose(out=ps_rtt[0:16, 512 + tc * 256 + 128:
                                                    512 + tc * 256 + 256],
                                         in_=rt_tm[:, tc * 192 + 160: tc * 192 + 176],
                                         identity=identf[:])
                    pe_rt.inc(tr)
                block.tensor(_t8)

                def _v8(v, tile=tile, pe_rt_now=pe_rt.val):
                    v.wait_ge(pe_rt.h, pe_rt_now)
                    last = None
                    for tc in range(2):
                        v.tensor_copy(out=cw_em[:, tile * 256 + tc * 128:
                                                tile * 256 + (tc + 1) * 128],
                                      in_=ps_rtt[0:32, 512 + tc * 256: 512 + tc * 256 + 128])
                        last = v.tensor_copy(out=acw_em[:, tile * 256 + tc * 128:
                                                        tile * 256 + (tc + 1) * 128],
                                             in_=ps_rtt[0:16, 512 + tc * 256 + 128:
                                                        512 + tc * 256 + 256])
                    dve_rt.inc(last)
                block.vector(_v8)
        R_.close()
        if KDBG and q == 0:
            _rtv = dve_rt.val
            def _gdbg2(gp, _rtv=_rtv):
                gp.wait_ge(dve_rt.h, _rtv)
                gp.dma_start(out=dbgb2_d[0:32, 0:1024], in_=cw_em[:]).then_inc(s_st[1].h, 16)
                gp.dma_start(out=dbgb2_d[0:16, 1024:2048], in_=acw_em[:]).then_inc(s_st[1].h, 16)
                s_st[1].val += 32
            block.gpsimd(_gdbg2)
        barrier()

        B_ = ExitStack()
        _bsb = lambda n, sh, dt: B_.enter_context(nc.sbuf_tensor(n, sh, dt))
        wgb0 = _bsb(f"wgb{q}_0", [128, 4096], BF16)
        wgb1 = _bsb(f"wgb{q}_1", [128, 4096], BF16)
        wgb2 = _bsb(f"wgb{q}_2", [128, 4096], BF16)
        wgb3 = _bsb(f"wgb{q}_3", [128, 4096], BF16)
        wub0 = _bsb(f"wub{q}_0", [128, 4096], BF16)
        wub1 = _bsb(f"wub{q}_1", [128, 4096], BF16)
        wub2 = _bsb(f"wub{q}_2", [128, 4096], BF16)
        wub3 = _bsb(f"wub{q}_3", [128, 4096], BF16)
        wdb0 = _bsb(f"wdb{q}_0", [128, 4096], BF16)
        wdb1 = _bsb(f"wdb{q}_1", [128, 4096], BF16)
        wdb2 = _bsb(f"wdb{q}_2", [128, 4096], BF16)
        wdb3 = _bsb(f"wdb{q}_3", [128, 4096], BF16)
        cwb_sb = _bsb(f"cwbs{q}", [128, 2048], BF16)
        silu_sb = _bsb(f"silu{q}", [128, 2048], F32)
        tmp_f = _bsb(f"tmpf{q}", [128, 1024], F32)
        hpr_sb = _bsb(f"hpr{q}", [128, 4096], BF16)
        oh32 = _bsb(f"oh32{q}", [32, 32 * 128], BF16)
        oh16 = _bsb(f"oh16{q}", [16, 16 * 128], BF16)
        ps_g = B_.enter_context(nc.psum_tensor(f"psg{q}", [128, 1024], F32))
        ps_u = B_.enter_context(nc.psum_tensor(f"psu{q}", [128, 1024], F32))
        ps_down = B_.enter_context(nc.psum_tensor(f"psdn{q}", [128, 2048], F32))
        if True:
            wgb = [wgb0, wgb1, wgb2, wgb3]
            wub = [wub0, wub1, wub2, wub3]
            wdb = [wdb0, wdb1, wdb2, wdb3]

            def emit_loads(slist, q=q):
                def _sl(sync, slist=tuple(slist), q=q):
                    for s_ in slist:
                        if q == 0:
                            csem, cval = cast_grp[s_]
                            sync.wait_ge(csem.h, cval)
                        kind, idx = subs[s_]
                        sg, su, sdn, cg, cu, cd, n_i = sub_srcs(kind, idx)
                        b = s_ % 4
                        if s_ >= 4:
                            sync.wait_ge(pe_down.h, down_after[s_ - 4])
                        sem = s_wb[b]
                        sync.dma_start(out=wgb[b][:, 0:8 * n_i], in_=cg).then_inc(sem.h, 16)
                        sem.val += 16
                        sync.dma_start(out=wub[b][:, 0:8 * n_i], in_=cu).then_inc(sem.h, 16)
                        sem.val += 16
                        sync.dma_start(out=wdb[b][:, 0:(n_i // 128) * 1024],
                                       in_=cd).then_inc(sem.h, 16)
                        sem.val += 16
                        wval[s_] = (sem, sem.val)
                block.sync(_sl)

            def _soh(sync):
                sync.dma_start(out=oh32[:], in_=oh32_d).then_inc(s_oh.h, 16)
                sync.dma_start(out=oh16[:], in_=oh16_d).then_inc(s_oh.h, 16)
                s_oh.val += 32
            block.sync(_soh)
            emit_loads([0, 1, 2, 3])
            next_load = 4

            for p, (s0, nsub) in enumerate(pair_of[:N_PAIR_EMIT]):
                ss = list(range(s0, s0 + nsub))
                has_cw = subs[s0][0] != 1
                act_b = act_silu.val
                dh_b = dve_h.val
                ev_b = dve_evac.val
                gu_b = pe_gu.val
                cwb_tgt = None

                def _tp(t, p=p, ss=tuple(ss), nsub=nsub, has_cw=has_cw,
                        act_b=act_b, dh_b=dh_b, ev_b=ev_b, cw_b=dve_cwb.val):
                    for s_ in ss:
                        sem, v_ = wval[s_]
                        t.wait_ge(sem.h, v_)
                    t.wait_ge(dve_evac.h, ev_b)
                    if has_cw:
                        t.wait_ge(dve_rt.h, dve_rt.val)
                        t.wait_ge(s_oh.h, s_oh.val)
                        for j, s_ in enumerate(ss):
                            kind, idx = subs[s_]
                            oh = oh32 if kind == 0 else oh16
                            src = cw_em if kind == 0 else acw_em
                            for tile in range(NTILE):
                                slot = j * 4 + tile
                                mm = t.matmul(out=ps_down[:, slot * 256:(slot + 1) * 256],
                                              lhsT=oh[:, idx * 128:(idx + 1) * 128],
                                              rhs=src[:, tile * 256:(tile + 1) * 256],
                                              start=True, stop=True)
                        pe_cwb.inc(mm)

                    def downs(tile):
                        t.wait_ge(dve_h.h, dh_b + tile * nsub + nsub)
                        t.wait_ge(dve_evac.h, ev_b + tile)
                        if has_cw and tile == 0:
                            t.wait_ge(dve_cwb.h, cw_b + 2)
                        for oc in range(8):
                            for j, s_ in enumerate(ss):
                                kind, idx = subs[s_]
                                n_ic = (MOE_I if kind != 2 else ANC_I) // 128
                                b = s_ % 4
                                for ic in range(n_ic):
                                    mm = t.matmul(out=ps_down[:, oc * 256:(oc + 1) * 256],
                                                  lhsT=wdb[b][:, ic * 1024 + oc * 128:
                                                              ic * 1024 + (oc + 1) * 128],
                                                  rhs=hpr_sb[:, (tile % 2) * 2048 + j * 1024
                                                             + ic * 256:
                                                             (tile % 2) * 2048 + j * 1024
                                                             + (ic + 1) * 256],
                                                  start=(j == 0 and ic == 0),
                                                  stop=(j == nsub - 1 and ic == n_ic - 1))
                        pe_down.inc(mm, nsub)

                    for tile in range(NTILE):
                        tok0 = tile * TILE
                        for j, s_ in enumerate(ss):
                            oi = tile * nsub + j
                            kind, idx = subs[s_]
                            n_i = MOE_I if kind != 2 else ANC_I
                            n_ich = n_i // 128
                            b = s_ % 4
                            t.wait_ge(act_silu.h, act_b + oi)
                            for ich in range(n_ich):
                                for kc in range(8):
                                    gmm = t.matmul(out=ps_g[:, ich * 256:(ich + 1) * 256],
                                                   lhsT=wgb[b][:, kc * n_i + ich * 128:
                                                               kc * n_i + (ich + 1) * 128],
                                                   rhs=y1qbf[:, kc * QTOK + tok0:
                                                             kc * QTOK + tok0 + TILE],
                                                   start=(kc == 0), stop=(kc == 7))
                            pe_g.inc(gmm)
                            t.wait_ge(dve_h.h, dh_b + oi)
                            for ich in range(n_ich):
                                for kc in range(8):
                                    mm = t.matmul(out=ps_u[:, ich * 256:(ich + 1) * 256],
                                                  lhsT=wub[b][:, kc * n_i + ich * 128:
                                                              kc * n_i + (ich + 1) * 128],
                                                  rhs=y1qbf[:, kc * QTOK + tok0:
                                                            kc * QTOK + tok0 + TILE],
                                                  start=(kc == 0), stop=(kc == 7))
                            pe_gu.inc(mm)
                        if tile >= 1:
                            downs(tile - 1)
                    downs(NTILE - 1)
                    for s_ in ss:
                        down_after[s_] = pe_down.val
                block.tensor(_tp)
                for s_ in ss:
                    down_after[s_] = pe_down.val


                def _ap(a, ss=tuple(ss), nsub=nsub, act_b=act_b, dh_b=dh_b, gu_b=gu_b,
                        pg_b=pe_g.val - 4 * nsub, has_cw=has_cw, cwb_now=pe_cwb.val):
                    if has_cw:
                        a.wait_ge(pe_cwb.h, cwb_now)
                        cc = a.activation(out=cwb_sb[:, 0:1024], in_=ps_down[:, 0:1024],
                                          func=AF.Identity)
                        dve_cwb.inc(cc, 1)
                    for tile in range(NTILE):
                        for j, s_ in enumerate(ss):
                            oi = tile * nsub + j
                            kind, idx = subs[s_]
                            n_i = MOE_I if kind != 2 else ANC_I
                            n_ich = n_i // 128
                            a.wait_ge(pe_g.h, pg_b + oi + 1)
                            if oi >= 2:
                                a.wait_ge(dve_h.h, dh_b + oi - 1)
                            sl = a.activation(out=silu_sb[:, (oi % 2) * 1024:
                                                          (oi % 2) * 1024 + n_ich * 256],
                                              in_=ps_g[:, 0:n_ich * 256], func=AF.Silu)
                            act_silu.inc(sl)
                block.scalar(_ap)

                if p + 1 < len(pair_of):
                    s0n, nsn = pair_of[p + 1]
                    want = list(range(next_load, s0n + nsn))
                    if want:
                        emit_loads(want)
                        next_load = want[-1] + 1

                pe_down_b = pe_down.val - 4 * nsub
                def _vp(v, p=p, ss=tuple(ss), nsub=nsub, has_cw=has_cw, act_b=act_b,
                        dh_b=dh_b, gu_b=gu_b, pe_down_b=pe_down_b, cwb_now=pe_cwb.val):
                    if has_cw:
                        v.wait_ge(pe_cwb.h, cwb_now)
                        cc = v.tensor_copy(out=cwb_sb[:, 1024:2048], in_=ps_down[:, 1024:2048])
                        dve_cwb.inc(cc, 1)
                    for tile in range(NTILE):
                        tok0 = tile * TILE
                        for j, s_ in enumerate(ss):
                            oi = tile * nsub + j
                            kind, idx = subs[s_]
                            n_i = MOE_I if kind != 2 else ANC_I
                            n_ich = n_i // 128
                            v.wait_ge(pe_gu.h, gu_b + oi + 1)
                            v.wait_ge(act_silu.h, act_b + oi + 1)
                            if tile >= 2:
                                v.wait_ge(pe_down.h, pe_down_b + (tile - 1) * nsub)
                            hbase = (tile % 2) * 2048 + j * 1024
                            v.tensor_mul(out=tmp_f[:, 0:n_ich * 256],
                                         in0=silu_sb[:, (oi % 2) * 1024:
                                                     (oi % 2) * 1024 + n_ich * 256],
                                         in1=ps_u[:, 0:n_ich * 256])
                            if kind != 1:
                                slot = j * 4 + tile
                                hh = v.tensor_tensor(
                                    out=hpr_sb[:, hbase:hbase + n_ich * 256]
                                        .rearrange("p (c t) -> p c t", t=256),
                                    in0=tmp_f[:, 0:n_ich * 256]
                                        .rearrange("p (c t) -> p c t", t=256),
                                    in1=cwb_sb[:, slot * 256:(slot + 1) * 256]
                                        .unsqueeze(1).broadcast_to((128, n_ich, 256)),
                                    op=ALU.mult)
                            else:
                                hh = v.tensor_copy(
                                    out=hpr_sb[:, hbase:hbase + n_ich * 256],
                                    in_=tmp_f[:, 0:n_ich * 256])
                            dve_h.inc(hh)
                        if tile >= 1:
                            et = tile - 1
                            etok = et * TILE
                            v.wait_ge(pe_down.h, pe_down_b + (et + 1) * nsub)
                            ev = v.tensor_tensor(
                                out=y1q[:].rearrange("p (c t) -> p c t", t=QTOK)
                                    [:, :, etok:etok + TILE],
                                in0=ps_down[:].rearrange("p (c t) -> p c t", t=256),
                                in1=y1q[:].rearrange("p (c t) -> p c t", t=QTOK)
                                    [:, :, etok:etok + TILE],
                                op=ALU.add)
                            dve_evac.inc(ev)
                    et = NTILE - 1
                    etok = et * TILE
                    v.wait_ge(pe_down.h, pe_down_b + (et + 1) * nsub)
                    ev = v.tensor_tensor(
                        out=y1q[:].rearrange("p (c t) -> p c t", t=QTOK)
                            [:, :, etok:etok + TILE],
                        in0=ps_down[:].rearrange("p (c t) -> p c t", t=256),
                        in1=y1q[:].rearrange("p (c t) -> p c t", t=QTOK)
                            [:, :, etok:etok + TILE],
                        op=ALU.add)
                    dve_evac.inc(ev)
                block.vector(_vp)

            barrier()
            B_.close()

        L_ = ExitStack()
        r2bf = L_.enter_context(nc.sbuf_tensor(f"r2bf{q}", [128, 2048], BF16))
        sq2 = L_.enter_context(nc.sbuf_tensor(f"sq2{q}", [128, 2048], BF16))
        rows2 = L_.enter_context(nc.sbuf_tensor(f"rows2{q}", [1, 1536], F32))
        out_tm = L_.enter_context(nc.sbuf_tensor(f"outtm{q}", [128, 2048], F32))
        ps_st2 = L_.enter_context(nc.psum_tensor(f"psst2{q}", [1, 512], F32))
        ps_bc2 = L_.enter_context(nc.psum_tensor(f"psbc2{q}", [128, 512], F32))
        ps_ot = L_.enter_context(nc.psum_tensor(f"psot{q}", [128, 1024], F32))
        if True:
            for tile in range(NTILE):
                tok0 = tile * TILE
                atok = qbase + tok0
                y1reg = y1q[:].rearrange("p (c t) -> p c t", t=QTOK)[:, :, tok0:tok0 + TILE]
                dve_ln_b = dve_ln.val
                act_ln_b = act_ln.val
                pe_ln_b = pe_ln.val

                def _vA(v, y1reg=y1reg):
                    ra = v.tensor_copy(out=r2bf[:].rearrange("p (c t) -> p c t", t=256),
                                       in_=y1reg)
                    dve_ln.inc(ra)
                block.vector(_vA)

                def _aA(a, y1reg=y1reg, dve_ln_b=dve_ln_b):
                    a.wait_ge(dve_ln.h, dve_ln_b + 1)
                    sq = a.activation(out=sq2[:].rearrange("p (c t) -> p c t", t=256),
                                      in_=y1reg, func=AF.Square)
                    act_ln.inc(sq)
                block.scalar(_aA)

                def _tA(t, tile=tile, dve_ln_b=dve_ln_b, act_ln_b=act_ln_b):
                    t.wait_ge(dve_ln.h, dve_ln_b + 1)
                    t.wait_ge(act_ln.h, act_ln_b + 1)
                    if tile > 0:
                        t.wait_ge(dve_ln.h, dve_ln_b)
                    for fc in range(8):
                        t.matmul(out=ps_st2[0:1, 0:256], lhsT=onesc[:, 0:1],
                                 rhs=r2bf[:, fc * 256:(fc + 1) * 256],
                                 start=(fc == 0), stop=(fc == 7))
                    for fc in range(8):
                        mm = t.matmul(out=ps_st2[0:1, 256:512], lhsT=onesc[:, 0:1],
                                      rhs=sq2[:, fc * 256:(fc + 1) * 256],
                                      start=(fc == 0), stop=(fc == 7))
                    pe_ln.inc(mm)
                block.tensor(_tA)

                def _vB(v, pe_ln_b=pe_ln_b, act_ln_b=act_ln_b):
                    v.wait_ge(pe_ln.h, pe_ln_b + 1)
                    mu = v.tensor_scalar(out=rows2[0:1, 1024:1280], in0=ps_st2[0:1, 0:256],
                                         scalar1=1.0 / D, scalar2=None, op0=ALU.mult)
                    dve_ln.inc(mu)
                    m2 = v.tensor_mul(out=rows2[0:1, 1280:1536], in0=rows2[0:1, 1024:1280],
                                      in1=rows2[0:1, 1024:1280])
                    dve_ln.inc(m2)
                    var = v.scalar_tensor_tensor(out=rows2[0:1, 0:256],
                                                 in0=ps_st2[0:1, 256:512],
                                                 scalar=1.0 / D, in1=rows2[0:1, 1280:1536],
                                                 op0=ALU.mult, op1=ALU.subtract)
                    dve_ln.inc(var)
                block.vector(_vB)

                def _aB(a, dve_ln_b=dve_ln_b):
                    a.wait_ge(dve_ln.h, dve_ln_b + 4)
                    a.activation(out=rows2[0:1, 256:512], in_=rows2[0:1, 0:256],
                                 func=AF.Ln, bias=eps5[0:1, :])
                    r = a.activation(out=rows2[0:1, 512:768], in_=rows2[0:1, 256:512],
                                     func=AF.Exp, scale=-0.5)
                    act_ln.inc(r)
                block.scalar(_aB)

                def _vC(v, act_ln_b=act_ln_b):
                    v.wait_ge(act_ln.h, act_ln_b + 2)
                    p_ = v.tensor_mul(out=rows2[0:1, 768:1024], in0=rows2[0:1, 1024:1280],
                                      in1=rows2[0:1, 512:768])
                    dve_ln.inc(p_)
                block.vector(_vC)

                def _tB(t, dve_ln_b=dve_ln_b):
                    t.wait_ge(dve_ln.h, dve_ln_b + 5)
                    t.matmul(out=ps_bc2[:, 0:256], lhsT=onesrf[:], rhs=rows2[0:1, 512:768],
                             start=True, stop=True)
                    mm = t.matmul(out=ps_bc2[:, 256:512], lhsT=onesrf[:],
                                  rhs=rows2[0:1, 768:1024], start=True, stop=True)
                    pe_ln.inc(mm)
                block.tensor(_tB)

                def _vD(v, y1reg=y1reg, pe_ln_b=pe_ln_b):
                    v.wait_ge(pe_ln.h, pe_ln_b + 2)
                    n1 = v.tensor_tensor(out=y1reg, in0=y1reg,
                                         in1=ps_bc2[:, 0:256].unsqueeze(1)
                                             .broadcast_to((128, 8, 256)),
                                         op=ALU.mult)
                    dve_ln.inc(n1)
                    n2 = v.tensor_tensor(out=y1reg, in0=y1reg,
                                         in1=ps_bc2[:, 256:512].unsqueeze(1)
                                             .broadcast_to((128, 8, 256)),
                                         op=ALU.subtract)
                    dve_ln.inc(n2)
                block.vector(_vD)

                for tc in range(2):
                    k = tile * 2 + tc
                    def _tC(t, tc=tc, tok0=tok0, dve_ln_now=dve_ln.val,
                            dve_ot_now=dve_ot.val):
                        t.wait_ge(dve_ln.h, dve_ln_now)
                        t.wait_ge(dve_ot.h, dve_ot_now)
                        for fc in range(8):
                            tr = t.transpose(
                                out=ps_ot[:, fc * 128:(fc + 1) * 128],
                                in_=y1q[:, fc * QTOK + tok0 + tc * 128:
                                        fc * QTOK + tok0 + (tc + 1) * 128],
                                identity=identf[:])
                        pe_ot.inc(tr)
                    block.tensor(_tC)

                    def _vE(v, k=k, pe_ot_now=pe_ot.val):
                        v.wait_ge(pe_ot.h, pe_ot_now)
                        sem = s_st[k % 2]
                        v.wait_ge(sem.h, sem.val)
                        ev = v.tensor_copy(out=out_tm[:, (k % 2) * 1024:(k % 2 + 1) * 1024],
                                           in_=ps_ot[:])
                        dve_ot.inc(ev)
                    block.vector(_vE)

                    def _gB(gp, k=k, atok=atok, tc=tc, dve_ot_now=dve_ot.val):
                        gp.wait_ge(dve_ot.h, dve_ot_now)
                        sem = s_st[k % 2]
                        gp.dma_start(out=out_d[seq, atok + tc * 128: atok + (tc + 1) * 128, :],
                                     in_=out_tm[:, (k % 2) * 1024:(k % 2 + 1) * 1024]
                                     ).then_inc(sem.h, 16)
                        sem.val += 16
                    block.gpsimd(_gB)

            barrier()
            L_.close()

    def _gfin(gp):
        for c in [s_const, s_x, s_scr, s_oh] + s_wA + s_wb + s_wc + s_st:
            if c.val:
                gp.wait_ge(c.h, c.val)
    block.gpsimd(_gfin)

    stk.close()
    return nc


_BF = ml_dtypes.bfloat16


def _consts():
    eye = np.eye(128, dtype=np.float32)
    oh32 = np.zeros((32, 32 * 128), dtype=np.float32)
    for e in range(32):
        oh32[e, e * 128:(e + 1) * 128] = 1.0
    oh16 = np.zeros((16, 16 * 128), dtype=np.float32)
    for l in range(16):
        oh16[l, l * 128:(l + 1) * 128] = 0.1
    return {
        "identb": eye.astype(_BF),
        "identf": eye,
        "maskT": np.triu(np.ones((128, 128), dtype=np.float32)),
        "onesc": np.ones((128, 64), dtype=np.float32).astype(_BF),
        "onesr": np.ones((1, 128), dtype=np.float32).astype(_BF),
        "onesrf": np.ones((1, 128), dtype=np.float32),
        "oh32": oh32.astype(_BF),
        "oh16": oh16.astype(_BF),
    }


def kernel(**inputs):
    nc = build_program()
    consts = _consts()
    base = {
        "Wq": inputs["L0_attn_Wq"], "bq": inputs["L0_attn_bq"],
        "Wk": inputs["L0_attn_Wk"], "bk": inputs["L0_attn_bk"],
        "Wv": inputs["L0_attn_Wv"], "bv": inputs["L0_attn_bv"],
        "Wo": inputs["L0_attn_Wo"], "bo": inputs["L0_attn_bo"],
        "gate_w": inputs["L0_ffn_gate_weight"], "gate_b": inputs["L0_ffn_gate_bias"],
        "w_gate": inputs["L0_ffn_w_gate"], "w_up": inputs["L0_ffn_w_up"],
        "w_down": inputs["L0_ffn_w_down"],
        "sw_gate": inputs["L0_ffn_shared_w_gate"], "sw_up": inputs["L0_ffn_shared_w_up"],
        "sw_down": inputs["L0_ffn_shared_w_down"],
        "aw_gate": inputs["L0_ffn_anchor_w_gate"], "aw_up": inputs["L0_ffn_anchor_w_up"],
        "aw_down": inputs["L0_ffn_anchor_w_down"],
    }
    base = {k: np.ascontiguousarray(np.asarray(v, dtype=np.float32)) for k, v in base.items()}
    base.update(consts)
    x = np.asarray(inputs["L0_attn_x"], dtype=np.float32)
    in_maps = []
    for c in range(8):
        m = dict(base)
        m["x"] = np.ascontiguousarray(x[2 * c:2 * c + 2])
        in_maps.append(m)
    res = run_bass_kernel_spmd(nc, in_maps, core_ids=list(range(8)))
    if "dbgb" in res.results[0]:
        np.save("/root/problem/_dev_dbgb.npy", np.asarray(res.results[0]["dbgb"], np.float32))
        np.save("/root/problem/_dev_dbgf.npy", np.asarray(res.results[0]["dbgf"]))
        np.save("/root/problem/_dev_dbgb2.npy", np.asarray(res.results[0]["dbgb2"], np.float32))
    out = np.concatenate([np.asarray(r["out"]) for r in res.results], axis=0)
    return out.astype(np.float32)

